# version 22
# baseline (speedup 1.0000x reference)
import numpy as np
import ml_dtypes
import concourse.bass as bass
import concourse.mybir as mybir
import concourse.alu_op_type as alu_mod
from concourse.bass_utils import run_bass_kernel_spmd

ALU = alu_mod.AluOpType
F32 = mybir.dt.float32
BF16 = mybir.dt.bfloat16
AF = mybir.ActivationFunctionType
AX = mybir.AxisListType

H = 2048
NH = 16
S = 2048
NT = 16
NG = 4
NC_ = 16
SCALE = 1.0 / float(np.sqrt(128.0))
EPS = 1e-5
PEEL = (128.0, 64.0, 32.0, 16.0, 8.0, 4.0, 2.0, 1.0)

ENGINES = ("tensor", "vector", "scalar", "gpsimd", "sync")


class _Op:
    __slots__ = ("engine", "fn", "deps", "inc", "sem", "cnt", "needed", "gate")

    def __init__(self, engine, fn, inc, sem, gate):
        self.engine = engine
        self.fn = fn
        self.deps = []
        self.inc = inc
        self.sem = sem
        self.cnt = None
        self.needed = True
        self.gate = gate


class Prog:

    NDMA = 8

    def __init__(self, nc):
        self.nc = nc
        self.ops = []
        self.sems = {}
        self.sem_names = ["sem_" + e for e in ENGINES] + \
                         [f"dma_{e}_{i}" for e in ENGINES for i in range(self.NDMA)]
        self.dma_idx = {e: 0 for e in ENGINES}
        self.last_w = {}
        self.readers = {}
        self.tail_waits = {}

    def op(self, engine, fn, reads=(), writes=(), dma=False):
        idx = len(self.ops)
        gate = None
        if dma:
            qi = self.dma_idx[engine]
            self.dma_idx[engine] += 1
            slot = qi % self.NDMA
            sem = f"dma_{engine}_{slot}"
            if qi >= self.NDMA:
                gate = (sem, (qi // self.NDMA) * 16)
            o = _Op(engine, fn, 16, sem, gate)
        else:
            o = _Op(engine, fn, 1, "sem_" + engine, None)
        deps = set()
        for r in reads:
            w = self.last_w.get(r)
            if w is not None:
                deps.add(w)
        for r in writes:
            w = self.last_w.get(r)
            if w is not None:
                deps.add(w)
            for rd in self.readers.get(r, ()):
                deps.add(rd)
        o.deps = [d for d in deps
                  if not (engine == "tensor" and self.ops[d].engine == "tensor")]
        for r in reads:
            self.readers.setdefault(r, []).append(idx)
        for r in writes:
            self.last_w[r] = idx
            self.readers[r] = []
        self.ops.append(o)

    def _finalize(self):
        for o in self.ops:
            if o.engine == "tensor" and o.inc == 1:
                o.needed = False
        for o in self.ops:
            by_sem = {}
            for d in o.deps:
                s = self.ops[d].sem
                if by_sem.get(s, -1) < d:
                    by_sem[s] = d
            o.deps = sorted(by_sem.values())
            for d in o.deps:
                self.ops[d].needed = True
        counts = {s: 0 for s in self.sem_names}
        for o in self.ops:
            if o.needed:
                counts[o.sem] += o.inc
                o.cnt = counts[o.sem]
        self.final_counts = counts

    def emit(self, block):
        self._finalize()

        def emit_engine(eng_obj, name):
            wm = {}
            for o in self.ops:
                if o.engine != name:
                    continue
                waits = []
                if o.gate is not None and o.gate[1] > 0:
                    waits.append(o.gate)
                for d in o.deps:
                    p = self.ops[d]
                    if p.cnt is not None:
                        waits.append((p.sem, p.cnt))
                for (s, v) in waits:
                    if wm.get(s, 0) < v:
                        wm[s] = v
                        eng_obj.wait_ge(self.sems[s], v)
                inst = o.fn(eng_obj)
                if o.needed:
                    inst.then_inc(self.sems[o.sem], o.inc)
            for (s, v) in self.tail_waits.get(name, ()):
                eng_obj.wait_ge(self.sems[s], v)

        used = {o.engine for o in self.ops} | set(self.tail_waits)
        if "tensor" in used:
            @block.tensor
            def _(t):
                emit_engine(t, "tensor")
        if "vector" in used:
            @block.vector
            def _(v):
                emit_engine(v, "vector")
        if "scalar" in used:
            @block.scalar
            def _(s):
                emit_engine(s, "scalar")
        if "gpsimd" in used:
            @block.gpsimd
            def _(g):
                emit_engine(g, "gpsimd")
        if "sync" in used:
            @block.sync
            def _(sy):
                emit_engine(sy, "sync")


def build():
    nc = bass.Bass("TRN2", target_bir_lowering=False)

    xin = nc.dram_tensor("x", [S, H], F32, kind="ExternalInput").ap()
    w_d = {nm: nc.dram_tensor(nm, [H, H], F32, kind="ExternalInput").ap()
           for nm in ("wq", "wk", "wv", "wo", "wf")}
    b_d = {nm: nc.dram_tensor(nm, [H], F32, kind="ExternalInput").ap()
           for nm in ("bq", "bk", "bv", "bo", "bf")}
    idf_d = nc.dram_tensor("idf", [128, 128], F32, kind="ExternalInput").ap()
    ones_d = nc.dram_tensor("onesb", [16, 512], BF16, kind="ExternalInput").ap()
    out_d = nc.dram_tensor("out", [S, H], F32, kind="ExternalOutput").ap()
    sc_oh = nc.dram_tensor("sc_oh", [16 * 128, 2048], BF16).ap()
    sc_ol = nc.dram_tensor("sc_ol", [16 * 128, 2048], BF16).ap()
    sc_r2 = nc.dram_tensor("sc_r2", [S, H], F32).ap()
    sc_b = nc.dram_tensor("sc_b", [16, 256], BF16).ap()
    sc_ws = nc.dram_tensor("sc_ws", [16, 256], BF16).ap()
    sc_wq = nc.dram_tensor("sc_wq", [S, H], F32).ap()

    def sb(name, shape, dt=F32):
        return nc.alloc_sbuf_tensor(name, shape, dt)

    lnxh = sb("lnxh", [128, NC_ * S], BF16)
    lnxl = sb("lnxl", [128, NC_ * S], BF16)
    wrow = sb("wrow", [128, H], F32)
    wqt = [sb(f"wqt{i}", [128, H], BF16) for i in range(4)]
    qh = sb("qh", [128, S], BF16)
    ql = sb("ql", [128, S], BF16)
    kh = sb("kh", [128, S], BF16)
    kl = sb("kl", [128, S], BF16)
    vtok = sb("vtok", [128, S], F32)
    pbuf = sb("pbuf", [128, S], F32)
    ptb = sb("ptb", [128, S], F32)
    oth = sb("oth", [128, 512], BF16)
    otl = sb("otl", [128, 512], BF16)
    fp0 = sb("fp0", [128, 512], F32)
    fp1 = sb("fp1", [128, 512], F32)
    fpu = sb("fpu", [128, 512], BF16)
    bpc = sb("bpc", [128, 512], F32)
    st = sb("st", [128, 64], F32)
    st2 = sb("st2", [128, 64], F32)
    st3 = sb("st3", [128, 64], F32)
    idf = sb("idf_s", [128, 128], F32)
    onesb = sb("onesb_s", [16, 512], BF16)
    bc4 = sb("bc4", [128, 64], F32)
    brow = sb("brow", [1, 1024], BF16)
    brow2 = sb("brow2", [1, 1024], BF16)

    psA = nc.alloc_psum_tensor("psA", [128, 2048], F32)
    psB = nc.alloc_psum_tensor("psB", [128, 2048], F32)
    ps3A = psA.ap().rearrange("p (c n) -> p c n", c=16)
    ps3B = psB.ap().rearrange("p (c n) -> p c n", c=16)
    lnh3 = lnxh.ap().rearrange("p (c n) -> p c n", c=NC_)
    lnl3 = lnxl.ap().rearrange("p (c n) -> p c n", c=NC_)

    p = Prog(nc)

    def K(*a):
        return tuple(a)

    PSa = [K("psA", j) for j in range(4)]
    PSb = [K("psB", j) for j in range(4)]

    def dma_in(dst_ap, src_ap, writes, reads=()):
        p.op("sync", lambda e, d=dst_ap, s=src_ap: e.dma_start(d, s),
             reads=reads, writes=writes, dma=True)

    def dma_out(dst_ap, src_ap, reads, writes=()):
        p.op("gpsimd", lambda e, d=dst_ap, s=src_ap: e.dma_start(d, s),
             reads=reads, writes=writes, dma=True)

    def mm(out_ap, lhsT_ap, rhs_ap, start, stop, reads, writes):
        p.op("tensor", lambda e, o=out_ap, l=lhsT_ap, r=rhs_ap, a=start, b=stop:
             e.matmul(o, l, r, start=a, stop=b), reads=reads, writes=writes)

    def tr(out_ap, in_ap, reads, writes):
        p.op("tensor", lambda e, o=out_ap, i=in_ap: e.transpose(o, i, idf[:, :]),
             reads=list(reads) + [K("idf")], writes=writes)

    def vcopy(out_ap, in_ap, reads, writes):
        p.op("vector", lambda e, o=out_ap, i=in_ap: e.tensor_copy(o, i),
             reads=reads, writes=writes)

    def scopy(out_ap, in_ap, reads, writes):
        p.op("scalar", lambda e, o=out_ap, i=in_ap: e.copy(o, i),
             reads=reads, writes=writes)

    def vsub(out_ap, a_ap, b_ap, reads, writes):
        p.op("vector", lambda e, o=out_ap, a=a_ap, b=b_ap: e.tensor_sub(o, a, b),
             reads=reads, writes=writes)

    def vmul(out_ap, a_ap, b_ap, reads, writes):
        p.op("vector", lambda e, o=out_ap, a=a_ap, b=b_ap: e.tensor_mul(o, a, b),
             reads=reads, writes=writes)

    def vadd(out_ap, a_ap, b_ap, reads, writes):
        p.op("vector", lambda e, o=out_ap, a=a_ap, b=b_ap: e.tensor_add(o, a, b),
             reads=reads, writes=writes)

    def stt(out_ap, in0_ap, scalar, in1_ap, op0, op1, reads, writes):
        p.op("vector", lambda e, o=out_ap, i0=in0_ap, sc=scalar, i1=in1_ap, a=op0, b=op1:
             e.scalar_tensor_tensor(o, i0, sc, i1, a, b), reads=reads, writes=writes)

    def ts(out_ap, in_ap, s1, s2, op0, op1, reads, writes, accum=None):
        if accum is None:
            p.op("vector", lambda e, o=out_ap, i=in_ap, a=s1, b=s2, x=op0, y=op1:
                 e.tensor_scalar(o, i, a, b, x, y), reads=reads, writes=writes)
        else:
            p.op("vector", lambda e, o=out_ap, i=in_ap, a=s1, b=s2, x=op0, y=op1, ac=accum:
                 e.tensor_scalar(o, i, a, b, x, y, accum_out=ac), reads=reads, writes=writes)

    def act(out_ap, in_ap, func, bias, scale, reads, writes, accum=None):
        if accum is None:
            p.op("scalar", lambda e, o=out_ap, i=in_ap, f=func, b=bias, s=scale:
                 e.activation(o, i, f, bias=b, scale=s), reads=reads, writes=writes)
        else:
            p.op("scalar", lambda e, o=out_ap, i=in_ap, f=func, b=bias, s=scale, ac=accum:
                 e.activation(o, i, f, bias=b, scale=s, accum_out=ac),
                 reads=reads, writes=writes)

    def reduce_absmax(out_ap, in_ap, reads, writes):
        p.op("vector", lambda e, o=out_ap, i=in_ap: e.reduce_max(
             o, i, axis=AX.X, apply_absolute_value=True), reads=reads, writes=writes)

    def recip(out_ap, in_ap, reads, writes):
        p.op("vector", lambda e, o=out_ap, i=in_ap: e.reciprocal(o, i),
             reads=reads, writes=writes)

    VELT = float(2 ** 29 + 1)

    def r32_guard(A, ka, B, kb, C, kc):
        p.op("scalar", lambda e, o=B, i=A: e.activation(
             o, i, AF.Identity, bias=0.0, scale=VELT), reads=[ka], writes=[kb])
        p.op("vector", lambda e, o=C, x=B, y=A: e.tensor_sub(o, x, y),
             reads=[kb, ka], writes=[kc])
        p.op("vector", lambda e, o=B, x=B, y=C: e.tensor_sub(o, x, y),
             reads=[kb, kc], writes=[kb])
        p.op("vector", lambda e, o=B, x=B, y=A: e.tensor_sub(o, x, y),
             reads=[kb, ka], writes=[kb])
        p.op("scalar", lambda e, o=C, i=A: e.activation(
             o, i, AF.Abs, bias=0.0, scale=float(2.0 ** -23)),
             reads=[ka], writes=[kc])
        p.op("vector", lambda e, o=B, x=B, y=C: e.tensor_tensor(o, x, y, ALU.min),
             reads=[kb, kc], writes=[kb])
        ts(C, C, -1.0, None, ALU.mult, ALU.bypass, reads=[kc], writes=[kc])
        p.op("vector", lambda e, o=B, x=B, y=C: e.tensor_tensor(o, x, y, ALU.max),
             reads=[kb, kc], writes=[kb])
        p.op("vector", lambda e, o=A, x=A, y=B: e.tensor_add(o, x, y),
             reads=[ka, kb], writes=[ka])

    dma_in(idf[:, :], idf_d[:, :], writes=[K("idf")])
    dma_in(onesb[:, :], ones_d[:, :], writes=[K("onesb")])
    for bi, nm in enumerate(("bq", "bk", "bv", "bo")):
        bl = bpc[0:16, 0:128]
        dma_in(bl, b_d[nm].rearrange("(a b) -> a b", b=128), writes=[K("bpc")])
        p.op("tensor", lambda e, o=psA[:, 0:16], i=bl: e.transpose(
             o, i, idf[0:16, 0:16]), reads=[K("bpc"), K("idf")], writes=[PSa[0]])
        vcopy(bc4[:, 16 * bi:16 * bi + 16], psA[:, 0:16],
              reads=[PSa[0]], writes=[K("bc4", bi)])
    blf = bpc[0:16, 0:128]
    dma_in(blf, b_d["bf"].rearrange("(a b) -> a b", b=128), writes=[K("bpc")])
    vcopy(fpu[0:16, 0:128], blf, reads=[K("bpc")], writes=[K("fpu")])
    p.op("vector", lambda e, o=fpu[0:16, 128:256], a=blf, b=fpu[0:16, 0:128]:
         e.tensor_sub(o, a, b), reads=[K("bpc"), K("fpu")], writes=[K("fpu")])
    dma_out(sc_b[:, :], fpu[0:16, 0:256], reads=[K("fpu")], writes=[K("sc_b")])

    def bias_col(bi, c):
        return bc4[:, 16 * bi + c:16 * bi + c + 1]

    p.op("gpsimd", lambda e: e.memset(bpc[:, 0:128], 0.0), writes=[K("bpc")])

    _slot = [0]

    def ln_stats(src_buf, read_keys):
        sl = _slot[0] % 2
        _slot[0] += 1
        bno = st2[:, 24 * sl:24 * sl + 24]
        for i in range(4):
            p.op("vector", lambda e, o=st2[:, 24 * sl + 6 * i:24 * sl + 6 * i + 6],
                 s=src_buf[:, 512 * i:512 * (i + 1)]: e.bn_stats(o, s),
                 reads=read_keys, writes=[K("bn", sl, i)])
        agg = st[:, 8 * sl:8 * sl + 2]
        p.op("vector", lambda e, o=agg, i=bno: e.bn_aggr(o, i),
             reads=[K("bn", sl, i) for i in range(4)], writes=[K("lnst", sl, "mv")])
        mean = agg[:, 0:1]
        var = agg[:, 1:2]
        t_ = st[:, 8 * sl + 2:8 * sl + 3]
        ts(t_, var, 1.0, EPS, ALU.mult, ALU.add,
           reads=[K("lnst", sl, "mv")], writes=[K("lnst", sl, "t")])
        lnt = st[:, 8 * sl + 3:8 * sl + 4]
        act(lnt, t_, AF.Ln, 0.0, 1.0, reads=[K("lnst", sl, "t")],
            writes=[K("lnst", sl, "lnt")])
        r0 = st[:, 8 * sl + 4:8 * sl + 5]
        act(r0, lnt, AF.Exp, 0.0, -0.5, reads=[K("lnst", sl, "lnt")],
            writes=[K("lnst", sl, "r0")])
        u_ = st[:, 8 * sl + 5:8 * sl + 6]
        vmul(u_, r0, r0, reads=[K("lnst", sl, "r0")], writes=[K("lnst", sl, "u")])
        vmul(u_, u_, t_, reads=[K("lnst", sl, "u"), K("lnst", sl, "t")],
             writes=[K("lnst", sl, "u")])
        ts(u_, u_, -0.5, 1.5, ALU.mult, ALU.add,
           reads=[K("lnst", sl, "u")], writes=[K("lnst", sl, "u")])
        r1 = st[:, 8 * sl + 6:8 * sl + 7]
        vmul(r1, r0, u_, reads=[K("lnst", sl, "r0"), K("lnst", sl, "u")],
             writes=[K("lnst", sl, "r1")])
        nb = st2[:, 48 + sl:49 + sl]
        vmul(nb, mean, r1, reads=[K("lnst", sl, "mv"), K("lnst", sl, "r1")],
             writes=[K("lnst", sl, "nb")])
        ts(nb, nb, -1.0, None, ALU.mult, ALU.bypass,
           reads=[K("lnst", sl, "nb")], writes=[K("lnst", sl, "nb")])
        return r1, nb, [K("lnst", sl, "r1"), K("lnst", sl, "nb")]

    def transpose_split(src_buf, src_keys, col, pp, dsth3=lnh3, dstl3=lnl3,
                        hname="lnxh", lname="lnxl"):
        ps = psA if pp == 0 else psB
        ps3 = ps3A if pp == 0 else ps3B
        pk = PSa if pp == 0 else PSb
        for c in range(16):
            tr(ps[:, 128 * c:128 * (c + 1)], src_buf[:, 128 * c:128 * (c + 1)],
               reads=src_keys, writes=[pk[c // 4]])
        for b in range(4):
            dh = dsth3[:, 4 * b:4 * b + 4, 128 * col:128 * (col + 1)]
            dl = dstl3[:, 4 * b:4 * b + 4, 128 * col:128 * (col + 1)]
            sp = ps3[:, 4 * b:4 * b + 4, :]
            hk = [K(hname, 4 * b + j, col) for j in range(4)]
            lk = [K(lname, 4 * b + j, col) for j in range(4)]
            p.op("scalar", lambda e, o=dh, i=sp: e.copy(o, i),
                 reads=[pk[b]], writes=hk)
            p.op("vector", lambda e, o=dl, i=sp, hh=dh: e.scalar_tensor_tensor(
                 o, i, 1.0, hh, ALU.mult, ALU.subtract),
                 reads=[pk[b]] + hk, writes=lk)

    WQQ = []

    def _w_p1(h, pc):
        wam = st3[:, 48:49]
        land = bpc[:, 256:512]
        dma_in(land, w_d["wf"][128 * h:128 * (h + 1), 256 * pc:256 * (pc + 1)],
               writes=[K("bpcw")])
        if pc == 0:
            reduce_absmax(wam, land, reads=[K("bpcw")], writes=[K("wam")])
        else:
            amp = st3[:, 53:54]
            reduce_absmax(amp, land, reads=[K("bpcw")], writes=[K("amp")])
            p.op("vector", lambda e, o=wam, a=wam, b=amp:
                 e.tensor_tensor(o, a, b, ALU.max),
                 reads=[K("wam"), K("amp")], writes=[K("wam")])
        if pc == 7:
            wscol = st3[:, h:h + 1]
            ts(wscol, wam, 1e-8, 1.0 / 127.0, ALU.max, ALU.mult,
               reads=[K("wam")], writes=[K("ws", h)])
            wrc = st3[:, 49:50]
            recip(wrc, wscol, reads=[K("ws", h)], writes=[K("wrc")])
            r32_guard(wscol, K("ws", h), st3[:, 51:52], K("sc51"),
                      st3[:, 52:53], K("sc52"))

    def _w_p2(h, pc):
        wrc = st3[:, 49:50]
        A, B = bpc[:, 0:256], bpc[:, 256:512]
        U = fpu[:, 0:256]
        dma_in(B, w_d["wf"][128 * h:128 * (h + 1), 256 * pc:256 * (pc + 1)],
               writes=[K("bpcw")])
        ts(A, B, wrc, 128.5, ALU.mult, ALU.add,
           reads=[K("bpcw"), K("wrc")], writes=[K("bpc")])
        scopy(B, A, reads=[K("bpc")], writes=[K("bpcw")])
        for hv in PEEL:
            ts(U, B, hv, hv, ALU.is_ge, ALU.mult,
               reads=[K("bpcw")], writes=[K("fpu")])
            vsub(B, B, U, reads=[K("bpcw"), K("fpu")], writes=[K("bpcw")])
        p.op("vector", lambda e, o=B, i=A, i1=B:
             e.scalar_tensor_tensor(o, i, -128.0, i1, ALU.add, ALU.subtract),
             reads=[K("bpc"), K("bpcw")], writes=[K("bpcw")])
        dma_out(sc_wq[128 * h:128 * (h + 1), 256 * pc:256 * (pc + 1)], B,
                reads=[K("bpcw")], writes=[K("sc_wq", h, pc // 2)])

    for h_ in range(NH):
        for pc_ in range(8):
            WQQ.append((0.9, (lambda h=h_, pc=pc_: _w_p1(h, pc))))
        for pc_ in range(8):
            WQQ.append((7.2, (lambda h=h_, pc=pc_: _w_p2(h, pc))))

    def wq_drain(budget_us):
        while WQQ and budget_us > 0:
            cost, em = WQQ.pop(0)
            em()
            budget_us -= cost

    def wq_drain_all():
        while WQQ:
            WQQ.pop(0)[1]()

    PBUF_ALL = [K("pbuf")] + [K("pbuf", j) for j in range(4)]
    for tt in range(NT):
        dma_in(pbuf[:, :], xin[128 * tt:128 * (tt + 1), :], writes=PBUF_ALL)
        s_ap, b_ap, skeys = ln_stats(pbuf, [K("pbuf")])
        act(ptb[:, :], pbuf[:, :], AF.Identity, b_ap, s_ap,
            reads=[K("pbuf")] + skeys, writes=[K("ptb")])
        transpose_split(ptb, [K("ptb")], tt, tt % 2)
        wq_drain(10.0)

    p.op("gpsimd", lambda e: e.memset(bpc[:, 0:128], 0.0),
         writes=[K("bpc")])

    LNX_ALL = [K("lnxh", c, t) for c in range(NC_) for t in range(NT)] + \
              [K("lnxl", c, t) for c in range(NC_) for t in range(NT)]

    def lnx_keys(g):
        ks = []
        for c in range(NC_):
            for t_ in range(4 * g, 4 * g + 4):
                ks.append(K("lnxh", c, t_))
                ks.append(K("lnxl", c, t_))
        return ks

    LNXG = [lnx_keys(g) for g in range(NG)]

    def wprep(dram, c, idx_h, idx_l, pp):
        dma_in(wrow[:, 0:1024], dram[128 * c:128 * (c + 1), 0:1024],
               writes=[K("wrow", 0)])
        dma_in(wrow[:, 1024:2048], dram[128 * c:128 * (c + 1), 1024:2048],
               writes=[K("wrow", 1)])
        ps = psA if pp == 0 else psB
        pk = PSa if pp == 0 else PSb
        for j in range(16):
            tr(ps[:, 128 * j:128 * (j + 1)], wrow[:, 128 * j:128 * (j + 1)],
               reads=[K("wrow", j // 8)], writes=[pk[j // 4]])
        dh, dl = wqt[idx_h], wqt[idx_l]
        for pc in range(4):
            sl = slice(512 * pc, 512 * (pc + 1))
            scopy(dh[:, sl], ps[:, sl], reads=[pk[pc]],
                  writes=[K("wqt", idx_h, pc)])
            p.op("vector", lambda e, o=dl[:, sl], i=ps[:, sl], hh=dh[:, sl]:
                 e.scalar_tensor_tensor(o, i, 1.0, hh, ALU.mult, ALU.subtract),
                 reads=[pk[pc], K("wqt", idx_h, pc)],
                 writes=[K("wqt", idx_l, pc)])

    def proj_tile(dst, dk, g, idx_h, idx_l):
        rk = LNXG[g]
        for c in range(NC_):
            lh = lnh3[:, c, 512 * g:512 * (g + 1)]
            ll = lnl3[:, c, 512 * g:512 * (g + 1)]
            whs = wqt[idx_h][:, 128 * c:128 * (c + 1)]
            wls = wqt[idx_l][:, 128 * c:128 * (c + 1)]
            kh_ = K("wqt", idx_h, c // 4)
            kl_ = K("wqt", idx_l, c // 4)
            mm(dst, whs, lh, c == 0, False, reads=[kh_] + (rk if c == 0 else []),
               writes=[dk])
            mm(dst, whs, ll, False, False, reads=[kh_], writes=[dk])
            mm(dst, wls, lh, False, c == NC_ - 1, reads=[kl_], writes=[dk])


    qcol = [0]
    for h in range(NH):
        pair_a = (0, 1) if h % 2 == 0 else (2, 3)
        pair_b = (2, 3) if h % 2 == 0 else (0, 1)
        wprep(w_d["wv"], h, pair_a[0], pair_a[1], 0)
        for g in range(NG):
            ps = psA if g % 2 == 0 else psB
            pk = PSa if g % 2 == 0 else PSb
            dst = ps[:, 0:512]
            proj_tile(dst, pk[0], g, pair_a[0], pair_a[1])
            stg, stgk = (fp0, K("fp0")) if g % 2 == 0 else (fp1, K("fp1"))
            act(stg[:, :], dst, AF.Identity, bias_col(2, h), 1.0,
                reads=[pk[0], K("bc4", 2)], writes=[stgk])
            ps2 = psB if g % 2 == 0 else psA
            pk2 = PSb if g % 2 == 0 else PSa
            for j in range(4):
                tr(ps2[:, 512 + 128 * j:512 + 128 * (j + 1)], stg[:, 128 * j:128 * (j + 1)],
                   reads=[stgk], writes=[pk2[1]])
            vcopy(vtok[:, 512 * g:512 * (g + 1)], ps2[:, 512:1024],
                  reads=[pk2[1]], writes=[K("vtok", g)])
        wprep(w_d["wq"], h, pair_b[0], pair_b[1], 0)
        for g in range(NG):
            ps = psA if g % 2 == 0 else psB
            pk = PSa if g % 2 == 0 else PSb
            dst = ps[:, 0:512]
            proj_tile(dst, pk[0], g, pair_b[0], pair_b[1])
            act(qh[:, 512 * g:512 * (g + 1)], dst, AF.Identity, bias_col(0, h), 1.0,
                reads=[pk[0], K("bc4", 0)], writes=[K("qh", g)])
            p.op("vector", lambda e, o=ql[:, 512 * g:512 * (g + 1)], i=dst,
                 hh=qh[:, 512 * g:512 * (g + 1)], bb=bias_col(0, h):
                 e.scalar_tensor_tensor(o, i, bb, hh, ALU.add, ALU.subtract),
                 reads=[pk[0], K("qh", g), K("bc4", 0)], writes=[K("ql", g)])
        wprep(w_d["wk"], h, pair_a[0], pair_a[1], 1)
        for g in range(NG):
            ps = psA if g % 2 == 0 else psB
            pk = PSa if g % 2 == 0 else PSb
            dst = ps[:, 0:512]
            proj_tile(dst, pk[0], g, pair_a[0], pair_a[1])
            act(kh[:, 512 * g:512 * (g + 1)], dst, AF.Identity, bias_col(1, h), 1.0,
                reads=[pk[0], K("bc4", 1)], writes=[K("kh", g)])
            p.op("vector", lambda e, o=kl[:, 512 * g:512 * (g + 1)], i=dst,
                 hh=kh[:, 512 * g:512 * (g + 1)], bb=bias_col(1, h):
                 e.scalar_tensor_tensor(o, i, bb, hh, ALU.add, ALU.subtract),
                 reads=[pk[0], K("kh", g), K("bc4", 1)], writes=[K("kl", g)])

        for qt in range(NT):
            ps = psA if qt % 2 == 0 else psB
            pk = PSa if qt % 2 == 0 else PSb
            lq_h = qh[:, 128 * qt:128 * (qt + 1)]
            lq_l = ql[:, 128 * qt:128 * (qt + 1)]
            qkeys = [K("qh", qt // 4), K("ql", qt // 4)]
            c0 = qcol[0] % 8
            qcol[0] += 1
            nm = st[:, 24 + c0:25 + c0]
            rc = st[:, 40 + c0:41 + c0]
            nmj = [st2[:, 50 + jj:51 + jj] for jj in range(4)]
            smj = [st2[:, 54 + jj:55 + jj] for jj in range(4)]
            for j in range(NG):
                dst = ps[:, 512 * j:512 * (j + 1)]
                rh = kh[:, 512 * j:512 * (j + 1)]
                rl = kl[:, 512 * j:512 * (j + 1)]
                kkeys = [K("kh", j), K("kl", j)]
                mm(dst, lq_h, rh, True, False, reads=qkeys + kkeys, writes=[pk[j]])
                mm(dst, lq_h, rl, False, False, reads=[], writes=[pk[j]])
                mm(dst, lq_l, rh, False, True, reads=[], writes=[pk[j]])
                ts(pbuf[:, 512 * j:512 * (j + 1)], dst, -SCALE, 3.0e38,
                   ALU.mult, ALU.min, reads=[pk[j]],
                   writes=[K("pbuf", j), K("nmj", j)], accum=nmj[j])
            p.op("vector", lambda e, o=nm, a=nmj[0], b=nmj[1]: e.tensor_tensor(
                 o, a, b, ALU.min), reads=[K("nmj", 0), K("nmj", 1)],
                 writes=[K("nm", c0)])
            p.op("vector", lambda e, o=nm, a=nm, b=nmj[2]: e.tensor_tensor(
                 o, a, b, ALU.min), reads=[K("nm", c0), K("nmj", 2)],
                 writes=[K("nm", c0)])
            p.op("vector", lambda e, o=nm, a=nm, b=nmj[3]: e.tensor_tensor(
                 o, a, b, ALU.min), reads=[K("nm", c0), K("nmj", 3)],
                 writes=[K("nm", c0)])
            for j in range(NG):
                act(pbuf[:, 512 * j:512 * (j + 1)], ps[:, 512 * j:512 * (j + 1)],
                    AF.Exp, nm, SCALE,
                    reads=[pk[j], K("nm", c0)],
                    writes=[K("pbuf", j), K("smj", j)], accum=smj[j])
            p.op("vector", lambda e, o=rc, a=smj[0], b=smj[1]: e.tensor_add(o, a, b),
                 reads=[K("smj", 0), K("smj", 1)], writes=[K("rc", c0)])
            p.op("vector", lambda e, o=rc, a=rc, b=smj[2]: e.tensor_add(o, a, b),
                 reads=[K("rc", c0), K("smj", 2)], writes=[K("rc", c0)])
            p.op("vector", lambda e, o=rc, a=rc, b=smj[3]: e.tensor_add(o, a, b),
                 reads=[K("rc", c0), K("smj", 3)], writes=[K("rc", c0)])
            recip(rc, rc, reads=[K("rc", c0)], writes=[K("rc", c0)])
            for j in range(16):
                tr(ps[:, 128 * j:128 * (j + 1)], pbuf[:, 128 * j:128 * (j + 1)],
                   reads=[K("pbuf", j // 4)], writes=[pk[j // 4]])
            scopy(ptb[:, :], ps[:, :], reads=[pk[0], pk[1], pk[2], pk[3]],
                  writes=[K("ptb")])
            vcopy(bpc[:, 0:1], rc, reads=[K("rc", c0)], writes=[K("bpc")])
            p.op("tensor", lambda e, o=ps[:, 256:384], i=bpc[:, 0:128]:
                 e.transpose(o, i, idf[:, :]),
                 reads=[K("bpc"), K("idf")], writes=[pk[0]])
            vcopy(brow2[0:1, 0:128], ps[0:1, 256:384], reads=[pk[0]],
                  writes=[K("brow2")])
            p.op("vector", lambda e, o=brow2[0:1, 128:256], i=ps[0:1, 256:384],
                 hh=brow2[0:1, 0:128]: e.scalar_tensor_tensor(
                 o, i, 1.0, hh, ALU.mult, ALU.subtract),
                 reads=[pk[0], K("brow2")], writes=[K("brow2")])
            mm(ps[:, 128:256], onesb[0:1, 0:128], brow2[0:1, 0:128], True, False,
               reads=[K("onesb"), K("brow2")], writes=[pk[0]])
            mm(ps[:, 128:256], onesb[0:1, 0:128], brow2[0:1, 128:256], False, True,
               reads=[K("brow2")], writes=[pk[0]])
            vcopy(bpc[:, 128:256], ps[:, 128:256], reads=[pk[0]], writes=[K("bpcrc")])
            first = True
            for j in range(NC_):
                mm(ps[:, 0:128], vtok[:, 128 * j:128 * (j + 1)],
                   ptb[:, 128 * j:128 * (j + 1)], first, j == NC_ - 1,
                   reads=([K("vtok", jj) for jj in range(4)] + [K("ptb")]) if first else [],
                   writes=[pk[0]])
                first = False
            q4 = qt % 4
            p.op("vector", lambda e, o=fp0[:, 0:128], a=ps[:, 0:128],
                 b=bpc[:, 128:256]: e.tensor_mul(o, a, b),
                 reads=[pk[0], K("bpcrc")], writes=[K("fp0")])
            vcopy(oth[:, 128 * q4:128 * (q4 + 1)], fp0[:, 0:128],
                  reads=[K("fp0")], writes=[K("oth", q4)])
            p.op("vector", lambda e, o=otl[:, 128 * q4:128 * (q4 + 1)],
                 i=fp0[:, 0:128], hh=oth[:, 128 * q4:128 * (q4 + 1)]:
                 e.scalar_tensor_tensor(o, i, 1.0, hh, ALU.mult, ALU.subtract),
                 reads=[K("fp0"), K("oth", q4)], writes=[K("otl", q4)])
            if q4 == 3:
                gq = qt // 4
                blk = (4 * gq + h // 4) * 128
                cs = 512 * (h % 4)
                dma_out(sc_oh[blk:blk + 128, cs:cs + 512],
                        oth[:, :], reads=[K("oth", i) for i in range(4)],
                        writes=[K("sc_oh", gq, h)])
                dma_out(sc_ol[blk:blk + 128, cs:cs + 512],
                        otl[:, :], reads=[K("otl", i) for i in range(4)],
                        writes=[K("sc_ol", gq, h)])

    for oc in range(NC_):
        dma_in(wrow[:, :], w_d["wo"][128 * oc:128 * (oc + 1), :], writes=[K("wrow"), K("wrow", 0), K("wrow", 1)])
        transpose_split(wrow, [K("wrow")], oc, oc % 2)

    ya = [pbuf, ptb, vtok, wrow]
    yak = ["pbuf", "ptb", "vtok", "wrow"]
    ogb = [qh, ql, kh, kl]
    ogbk = ["qh", "ql", "kh", "kl"]

    def ya_keys(t_):
        if yak[t_] == "pbuf":
            return [K("pbuf")] + [K("pbuf", j) for j in range(4)]
        if yak[t_] == "wrow":
            return [K("wrow"), K("wrow", 0), K("wrow", 1)]
        if yak[t_] == "ptb":
            return [K("ptb")]
        return [K("vtok", i) for i in range(4)]

    for g in range(NG):
        for b in range(4):
            blk = (4 * g + b) * 128
            dma_in(ogb[b][:, :], sc_oh[blk:blk + 128, :],
                   writes=[K(ogbk[b], q) for q in range(4)],
                   reads=[K("sc_oh", g, 4 * b + q) for q in range(4)])
            dma_in(wqt[b][:, :], sc_ol[blk:blk + 128, :],
                   writes=[K("wqt", b, q) for q in range(4)],
                   reads=[K("sc_ol", g, 4 * b + q) for q in range(4)])

        def oglo_key(c):
            return K("wqt", c // 4, c % 4)

        for ocg in range(4):
            for oc2 in range(4):
                oc = 4 * ocg + oc2
                dst = psA[:, 512 * (oc2 % 2):512 * (oc2 % 2) + 512]
                dk = PSa[oc2 % 2]
                wo_keys = [K("lnxh", c_, oc) for c_ in range(NC_)] + \
                          [K("lnxl", c_, oc) for c_ in range(NC_)]
                for c in range(NC_):
                    ah = ogb[c // 4][:, 512 * (c % 4):512 * (c % 4 + 1)]
                    al_ = wqt[c // 4][:, 512 * (c % 4):512 * (c % 4 + 1)]
                    akeys = [K(ogbk[c // 4], c % 4), oglo_key(c)]
                    wh_ = lnh3[:, c, 128 * oc:128 * (oc + 1)]
                    wl_ = lnl3[:, c, 128 * oc:128 * (oc + 1)]
                    mm(dst, wh_, ah, c == 0, False,
                       reads=(wo_keys if c == 0 else []) + akeys, writes=[dk])
                    mm(dst, wh_, al_, False, False, reads=[], writes=[dk])
                    mm(dst, wl_, ah, False, c == NC_ - 1, reads=[], writes=[dk])
                stg, stgk = (fp0, K("fp0")) if oc2 % 2 == 0 else (fp1, K("fp1"))
                act(stg[:, :], dst, AF.Identity, bias_col(3, oc), 1.0,
                    reads=[dk, K("bc4", 3)], writes=[stgk])
                for t_ in range(4):
                    tr(psB[:, 512 * t_ + 128 * oc2:512 * t_ + 128 * (oc2 + 1)],
                       stg[:, 128 * t_:128 * (t_ + 1)], reads=[stgk], writes=[PSb[t_]])
            for t_ in range(4):
                vcopy(ya[t_][:, 512 * ocg:512 * (ocg + 1)],
                      psB[:, 512 * t_:512 * (t_ + 1)],
                      reads=[PSb[t_]], writes=ya_keys(t_))
            wq_drain(16.0)
        for t_ in range(4):
            tt = 4 * g + t_
            yk = ya_keys(t_)
            s_ap, b_ap, skeys = ln_stats(ya[t_], yk)
            for h2 in range(2):
                sl = slice(1024 * h2, 1024 * (h2 + 1))
                bks = [PSb[2 * h2], PSb[2 * h2 + 1]]
                act(psB[:, sl], ya[t_][:, sl], AF.Identity, b_ap, s_ap,
                    reads=yk + skeys, writes=bks)
                dma_in(ya[t_][:, sl], xin[128 * tt:128 * (tt + 1), sl],
                       writes=yk, reads=bks)
                p.op("vector", lambda e, o=ya[t_][:, sl], a=ya[t_][:, sl],
                     b=psB[:, sl]: e.tensor_add(o, a, b),
                     reads=yk + bks, writes=yk)
                dma_out(sc_r2[128 * tt:128 * (tt + 1), sl], ya[t_][:, sl],
                        reads=yk, writes=[K("sc_r2", tt, 2 * h2), K("sc_r2", tt, 2 * h2 + 1)])
            wq_drain(10.0)

    def quant_peel(src_buf, src_keys, rcol, rkey, col, hname_lname, pp,
                   eng="vector", bufs=None):
        d3 = lnh3 if hname_lname == "lnxh" else lnl3
        ps = psA if pp == 0 else psB
        ps_ = ps3A if pp == 0 else ps3B
        pk = PSa if pp == 0 else PSb
        if bufs is None:
            A, kA, B, kB, U, kU, W_ = fp0, K("fp0"), fp1, K("fp1"), fpu, K("fpu"), 512
        else:
            A, kA, B, kB, U, kU, W_ = bufs
        npc = 2048 // W_

        def tseng(out_ap, in_ap, s1, s2, op0, op1, reads, writes):
            p.op(eng, lambda e, o=out_ap, i=in_ap, a=s1, b=s2, x=op0, y=op1:
                 e.tensor_scalar(o, i, a, b, x, y), reads=reads, writes=writes)

        for pc in range(npc):
            sl = src_buf[:, W_ * pc:W_ * (pc + 1)]
            Asl, Bsl, Usl = A[:, 0:W_], B[:, 0:W_], U[:, 0:W_]
            tseng(Asl, sl, rcol, 128.5, ALU.mult, ALU.add,
                  reads=list(src_keys) + [rkey], writes=[kA])
            p.op("scalar" if eng == "vector" else eng,
                 lambda e, o=Bsl, i=Asl: e.copy(o, i) if hasattr(e, "copy")
                 else e.tensor_copy(o, i), reads=[kA], writes=[kB])
            for hv in PEEL:
                tseng(Usl, Bsl, hv, hv, ALU.is_ge, ALU.mult,
                      reads=[kB], writes=[kU])
                p.op(eng, lambda e, o=Bsl, a=Bsl, b=Usl: e.tensor_sub(o, a, b),
                     reads=[kB, kU], writes=[kB])
            if eng == "vector":
                p.op(eng, lambda e, o=Bsl, i=Asl, i1=Bsl:
                     e.scalar_tensor_tensor(o, i, -128.0, i1, ALU.add, ALU.subtract),
                     reads=[kA, kB], writes=[kB])
            else:
                tseng(Asl, Asl, -128.0, None, ALU.add, ALU.bypass,
                      reads=[kA], writes=[kA])
                p.op(eng, lambda e, o=Bsl, a=Asl, b=Bsl: e.tensor_sub(o, a, b),
                     reads=[kA, kB], writes=[kB])
            for j in range(W_ // 128):
                tr(ps[:, W_ * pc + 128 * j:W_ * pc + 128 * (j + 1)],
                   Bsl[:, 128 * j:128 * (j + 1)], reads=[kB],
                   writes=[pk[(W_ * pc + 128 * j) // 512]])
        for bq in range(4):
            dh = d3[:, 4 * bq:4 * bq + 4, 128 * col:128 * (col + 1)]
            hk = [K(hname_lname, 4 * bq + j, col) for j in range(4)]
            p.op("scalar", lambda e, o=dh, i=ps_[:, 4 * bq:4 * bq + 4, :]:
                 e.copy(o, i), reads=[pk[bq]], writes=hk)

    wq_drain_all()
    for oc in range(NC_):
        dma_in(wrow[:, :], sc_wq[128 * oc:128 * (oc + 1), :],
               writes=[K("wrow"), K("wrow", 0), K("wrow", 1)],
               reads=[K("sc_wq", oc, q) for q in range(4)])
        for j in range(16):
            tr(psA[:, 128 * j:128 * (j + 1)], wrow[:, 128 * j:128 * (j + 1)],
               reads=[K("wrow"), K("wrow", 0), K("wrow", 1)],
               writes=[PSa[j // 4]])
        for bq in range(4):
            dh = lnl3[:, 4 * bq:4 * bq + 4, 128 * oc:128 * (oc + 1)]
            hk = [K("lnxl", 4 * bq + j, oc) for j in range(4)]
            p.op("scalar", lambda e, o=dh, i=ps3A[:, 4 * bq:4 * bq + 4, :]:
                 e.copy(o, i), reads=[PSa[bq]], writes=hk)

    p.op("gpsimd", lambda e: e.memset(fp0[:, 0:128], 0.0), writes=[K("fp0")])
    vcopy(fp0[:, 0:16], st3[:, 0:16], reads=[K("ws", oc_) for oc_ in range(NC_)],
          writes=[K("fp0")])
    p.op("tensor", lambda e: e.transpose(psA[:, 0:128], fp0[:, 0:128], idf[:, :]),
         reads=[K("fp0"), K("idf")], writes=[PSa[0]])
    vcopy(fpu[0:16, 0:128], psA[0:16, 0:128], reads=[PSa[0]], writes=[K("fpu")])
    p.op("vector", lambda e: e.scalar_tensor_tensor(
         fpu[0:16, 128:256], psA[0:16, 0:128], 1.0, fpu[0:16, 0:128],
         ALU.mult, ALU.subtract), reads=[PSa[0], K("fpu")], writes=[K("fpu")])
    dma_out(sc_ws[:, :], fpu[0:16, 0:256], reads=[K("fpu")], writes=[K("sc_ws")])

    for tt in range(NT):
        dma_in(pbuf[:, :], sc_r2[128 * tt:128 * (tt + 1), :], writes=PBUF_ALL,
               reads=[K("sc_r2", tt, pc_) for pc_ in range(4)])
        s_ap, b_ap, skeys = ln_stats(pbuf, [K("pbuf")])
        act(ptb[:, :], pbuf[:, :], AF.Identity, b_ap, s_ap,
            reads=[K("pbuf")] + skeys, writes=[K("ptb")])
        xam = st3[:, 50:51]
        reduce_absmax(xam, ptb[:, :], reads=[K("ptb")], writes=[K("xam")])
        xscol = st3[:, 16 + tt:17 + tt]
        ts(xscol, xam, 1.0 / 127.0, 1e-12, ALU.mult, ALU.max,
           reads=[K("xam")], writes=[K("xs", tt)])
        xrcol = st3[:, 32 + tt:33 + tt]
        recip(xrcol, xscol, reads=[K("xs", tt)], writes=[K("xr", tt)])
        quant_peel(ptb, [K("ptb")], xrcol, K("xr", tt), tt, "lnxh", 1)
        xs32col = st[:, 48 + tt:49 + tt]
        vcopy(xs32col, xscol, reads=[K("xs", tt)], writes=[K("xs32", tt)])
        r32_guard(xs32col, K("xs32", tt), st3[:, 53:54], K("sc53"),
                  st3[:, 54:55], K("sc54"))

    for g in range(NG):
        for ocg in range(4):
            dma_in(brow[0:1, :], sc_b[4 * ocg:4 * ocg + 4, :].rearrange("a b -> (a b)").rearrange("(a b) -> a b", a=1),
                   writes=[K("brow")], reads=[K("sc_b")])
            for oc2 in range(4):
                dstb = psA[:, 1536 + 128 * oc2:1536 + 128 * (oc2 + 1)]
                mm(dstb, onesb[0:1, 0:128], brow[0:1, 256 * oc2:256 * oc2 + 128],
                   True, False, reads=[K("onesb"), K("brow")], writes=[PSa[3]])
                mm(dstb, onesb[0:1, 0:128], brow[0:1, 256 * oc2 + 128:256 * oc2 + 256],
                   False, True, reads=[K("onesb"), K("brow")], writes=[PSa[3]])
            vcopy(bpc[:, :], psA[:, 1536:2048], reads=[PSa[3]],
                  writes=[K("bpc"), K("bpcrc"), K("bpcw")])
            dma_in(brow2[0:1, :], sc_ws[4 * ocg:4 * ocg + 4, :].rearrange("a b -> (a b)").rearrange("(a b) -> a b", a=1),
                   writes=[K("brow2")], reads=[K("sc_ws")])
            for oc2 in range(4):
                wdst = psA[:, 1024 + 128 * oc2:1024 + 128 * (oc2 + 1)]
                mm(wdst, onesb[0:1, 0:128], brow2[0:1, 256 * oc2:256 * oc2 + 128],
                   True, False, reads=[K("onesb"), K("brow2")], writes=[PSa[2]])
                mm(wdst, onesb[0:1, 0:128], brow2[0:1, 256 * oc2 + 128:256 * oc2 + 256],
                   False, True, reads=[K("onesb"), K("brow2")], writes=[PSa[2]])
            for oc2 in range(4):
                oc = 4 * ocg + oc2
                dst = psA[:, 512 * (oc2 % 2):512 * (oc2 % 2) + 512]
                dk = PSa[oc2 % 2]
                first = True
                for c in range(NC_):
                    wslice = lnl3[:, c, 128 * oc:128 * (oc + 1)]
                    xslice = lnh3[:, c, 512 * g:512 * (g + 1)]
                    rks = [K("lnxl", c, oc)] + \
                          [K("lnxh", c, 4 * g + i) for i in range(4)]
                    mm(dst, wslice, xslice, first, c == NC_ - 1,
                       reads=rks, writes=[dk])
                    first = False
                stg, stgk = (fp0, K("fp0")) if oc2 % 2 == 0 else (fp1, K("fp1"))
                scopy(stg[:, :], dst, reads=[dk], writes=[stgk])
                for t_ in range(4):
                    tr(psB[:, 512 * t_ + 128 * oc2:512 * t_ + 128 * (oc2 + 1)],
                       stg[:, 128 * t_:128 * (t_ + 1)], reads=[stgk], writes=[PSb[t_]])
            for t_ in range(4):
                tt = 4 * g + t_
                xs32c = st[:, 48 + tt:49 + tt]
                yasl = ya[t_][:, 512 * ocg:512 * (ocg + 1)]
                yk = ya_keys(t_)
                p.op("scalar", lambda e, o=fp0[:, :], i=psB[:, 512 * t_:512 * (t_ + 1)],
                     sc=xs32c: e.activation(o, i, AF.Identity, bias=0.0, scale=sc),
                     reads=[PSb[t_], K("xs32", tt)], writes=[K("fp0")])
                r32_guard(fp0[:, :], K("fp0"), fp1[:, :], K("fp1"), yasl, yk[0])
                p.op("vector", lambda e, o=fp1[:, :], a=fp0[:, :],
                     b=psA[:, 1024:1536]: e.tensor_mul(o, a, b),
                     reads=[K("fp0"), PSa[2]], writes=[K("fp1")])
                r32_guard(fp1[:, :], K("fp1"), fp0[:, :], K("fp0"), yasl, yk[0])
                p.op("vector", lambda e, o=yasl, a=fp1[:, :], b=bpc[:, :]:
                     e.tensor_add(o, a, b),
                     reads=[K("fp1"), K("bpc")], writes=yk)
        for t_ in range(4):
            tt = 4 * g + t_
            yk = ya_keys(t_)
            s_ap, b_ap, skeys = ln_stats(ya[t_], yk)
            for h2 in range(2):
                sl = slice(1024 * h2, 1024 * (h2 + 1))
                bks = [PSb[2 * h2], PSb[2 * h2 + 1]]
                act(psB[:, sl], ya[t_][:, sl], AF.Identity, b_ap, s_ap,
                    reads=yk + skeys, writes=bks)
                dma_in(ya[t_][:, sl], sc_r2[128 * tt:128 * (tt + 1), sl],
                       writes=yk, reads=bks + [K("sc_r2", tt, 2 * h2), K("sc_r2", tt, 2 * h2 + 1)])
                p.op("vector", lambda e, o=ya[t_][:, sl], a=ya[t_][:, sl],
                     b=psB[:, sl]: e.tensor_add(o, a, b),
                     reads=yk + bks, writes=yk)
                dma_out(out_d[128 * tt:128 * (tt + 1), sl], ya[t_][:, sl],
                        reads=yk)

    ng_ = p.dma_idx["gpsimd"]
    p.tail_waits = {"gpsimd": [(f"dma_gpsimd_{i}",
                                ((ng_ - 1 - i) // Prog.NDMA + 1) * 16)
                               for i in range(min(Prog.NDMA, ng_))]}

    import contextlib
    stack = contextlib.ExitStack()
    handles = {}
    for s_ in p.sem_names:
        handles[s_] = stack.enter_context(nc.semaphore(s_))
    p.sems = handles
    with nc.Block() as block:
        p.emit(block)
    stack.close()
    return nc


_CACHE = {}


def kernel(L0_attn_hidden_state, L0_attn_Wq, L0_attn_bq, L0_attn_Wk, L0_attn_bk,
           L0_attn_Wv, L0_attn_bv, L0_attn_Wo, L0_attn_bo, L0_ffn_weight, L0_ffn_bias):
    if "nc" not in _CACHE:
        _CACHE["nc"] = build()
    nc = _CACHE["nc"]
    hs = np.ascontiguousarray(np.asarray(L0_attn_hidden_state, dtype=np.float32))
    common = {
        "wq": np.ascontiguousarray(L0_attn_Wq, dtype=np.float32),
        "wk": np.ascontiguousarray(L0_attn_Wk, dtype=np.float32),
        "wv": np.ascontiguousarray(L0_attn_Wv, dtype=np.float32),
        "wo": np.ascontiguousarray(L0_attn_Wo, dtype=np.float32),
        "wf": np.ascontiguousarray(L0_ffn_weight, dtype=np.float32),
        "bq": np.ascontiguousarray(L0_attn_bq, dtype=np.float32),
        "bk": np.ascontiguousarray(L0_attn_bk, dtype=np.float32),
        "bv": np.ascontiguousarray(L0_attn_bv, dtype=np.float32),
        "bo": np.ascontiguousarray(L0_attn_bo, dtype=np.float32),
        "bf": np.ascontiguousarray(L0_ffn_bias, dtype=np.float32),
        "idf": np.eye(128, dtype=np.float32),
        "onesb": np.ones((16, 512), dtype=ml_dtypes.bfloat16),
    }
    in_maps = [dict(common, x=np.ascontiguousarray(hs[b])) for b in range(8)]
    res = run_bass_kernel_spmd(nc, in_maps, core_ids=list(range(8)))
    out = np.stack([np.asarray(r["out"]) for r in res.results], axis=0)
    return out


# version 23
# speedup vs baseline: 1.0804x; 1.0804x over previous
import numpy as np
import ml_dtypes
import concourse.bass as bass
import concourse.mybir as mybir
import concourse.alu_op_type as alu_mod
from concourse.bass_utils import run_bass_kernel_spmd

ALU = alu_mod.AluOpType
F32 = mybir.dt.float32
BF16 = mybir.dt.bfloat16
AF = mybir.ActivationFunctionType
AX = mybir.AxisListType

H = 2048
NH = 16
S = 2048
NT = 16
NG = 4
NC_ = 16
SCALE = 1.0 / float(np.sqrt(128.0))
EPS = 1e-5
PEEL = (128.0, 64.0, 32.0, 16.0, 8.0, 4.0, 2.0, 1.0)

ENGINES = ("tensor", "vector", "scalar", "gpsimd", "sync")


class _Op:
    __slots__ = ("engine", "fn", "deps", "inc", "sem", "cnt", "needed", "gate")

    def __init__(self, engine, fn, inc, sem, gate):
        self.engine = engine
        self.fn = fn
        self.deps = []
        self.inc = inc
        self.sem = sem
        self.cnt = None
        self.needed = True
        self.gate = gate


class Prog:

    NDMA = 8

    def __init__(self, nc):
        self.nc = nc
        self.ops = []
        self.sems = {}
        self.sem_names = ["sem_" + e for e in ENGINES] + \
                         [f"dma_{e}_{i}" for e in ENGINES for i in range(self.NDMA)]
        self.dma_idx = {e: 0 for e in ENGINES}
        self.last_w = {}
        self.readers = {}
        self.tail_waits = {}

    def op(self, engine, fn, reads=(), writes=(), dma=False):
        idx = len(self.ops)
        gate = None
        if dma:
            qi = self.dma_idx[engine]
            self.dma_idx[engine] += 1
            slot = qi % self.NDMA
            sem = f"dma_{engine}_{slot}"
            if qi >= self.NDMA:
                gate = (sem, (qi // self.NDMA) * 16)
            o = _Op(engine, fn, 16, sem, gate)
        else:
            o = _Op(engine, fn, 1, "sem_" + engine, None)
        deps = set()
        for r in reads:
            w = self.last_w.get(r)
            if w is not None:
                deps.add(w)
        for r in writes:
            w = self.last_w.get(r)
            if w is not None:
                deps.add(w)
            for rd in self.readers.get(r, ()):
                deps.add(rd)
        o.deps = [d for d in deps
                  if not (engine == "tensor" and self.ops[d].engine == "tensor")]
        for r in reads:
            self.readers.setdefault(r, []).append(idx)
        for r in writes:
            self.last_w[r] = idx
            self.readers[r] = []
        self.ops.append(o)

    def _finalize(self):
        for o in self.ops:
            if o.engine == "tensor" and o.inc == 1:
                o.needed = False
        for o in self.ops:
            by_sem = {}
            for d in o.deps:
                s = self.ops[d].sem
                if by_sem.get(s, -1) < d:
                    by_sem[s] = d
            o.deps = sorted(by_sem.values())
            for d in o.deps:
                self.ops[d].needed = True
        counts = {s: 0 for s in self.sem_names}
        for o in self.ops:
            if o.needed:
                counts[o.sem] += o.inc
                o.cnt = counts[o.sem]
        self.final_counts = counts

    def emit(self, block):
        self._finalize()

        def emit_engine(eng_obj, name):
            wm = {}
            for o in self.ops:
                if o.engine != name:
                    continue
                waits = []
                if o.gate is not None and o.gate[1] > 0:
                    waits.append(o.gate)
                for d in o.deps:
                    p = self.ops[d]
                    if p.cnt is not None:
                        waits.append((p.sem, p.cnt))
                for (s, v) in waits:
                    if wm.get(s, 0) < v:
                        wm[s] = v
                        eng_obj.wait_ge(self.sems[s], v)
                inst = o.fn(eng_obj)
                if o.needed:
                    inst.then_inc(self.sems[o.sem], o.inc)
            for (s, v) in self.tail_waits.get(name, ()):
                eng_obj.wait_ge(self.sems[s], v)

        used = {o.engine for o in self.ops} | set(self.tail_waits)
        if "tensor" in used:
            @block.tensor
            def _(t):
                emit_engine(t, "tensor")
        if "vector" in used:
            @block.vector
            def _(v):
                emit_engine(v, "vector")
        if "scalar" in used:
            @block.scalar
            def _(s):
                emit_engine(s, "scalar")
        if "gpsimd" in used:
            @block.gpsimd
            def _(g):
                emit_engine(g, "gpsimd")
        if "sync" in used:
            @block.sync
            def _(sy):
                emit_engine(sy, "sync")


def build():
    nc = bass.Bass("TRN2", target_bir_lowering=False)

    xin = nc.dram_tensor("x", [S, H], F32, kind="ExternalInput").ap()
    w_d = {nm: nc.dram_tensor(nm, [H, H], F32, kind="ExternalInput").ap()
           for nm in ("wq", "wk", "wv", "wo", "wf")}
    b_d = {nm: nc.dram_tensor(nm, [H], F32, kind="ExternalInput").ap()
           for nm in ("bq", "bk", "bv", "bo", "bf")}
    idf_d = nc.dram_tensor("idf", [128, 128], F32, kind="ExternalInput").ap()
    ones_d = nc.dram_tensor("onesb", [16, 512], BF16, kind="ExternalInput").ap()
    out_d = nc.dram_tensor("out", [S, H], F32, kind="ExternalOutput").ap()
    sc_oh = nc.dram_tensor("sc_oh", [16 * 128, 2048], BF16).ap()
    sc_ol = nc.dram_tensor("sc_ol", [16 * 128, 2048], BF16).ap()
    sc_r2 = nc.dram_tensor("sc_r2", [S, H], F32).ap()
    sc_b = nc.dram_tensor("sc_b", [16, 256], BF16).ap()
    sc_ws = nc.dram_tensor("sc_ws", [16, 256], BF16).ap()
    sc_wq = nc.dram_tensor("sc_wq", [S, H], F32).ap()

    def sb(name, shape, dt=F32):
        return nc.alloc_sbuf_tensor(name, shape, dt)

    lnxh = sb("lnxh", [128, NC_ * S], BF16)
    lnxl = sb("lnxl", [128, NC_ * S], BF16)
    wrow = sb("wrow", [128, H], F32)
    wqt = [sb(f"wqt{i}", [128, H], BF16) for i in range(4)]
    qh = sb("qh", [128, S], BF16)
    ql = sb("ql", [128, S], BF16)
    kh = sb("kh", [128, S], BF16)
    kl = sb("kl", [128, S], BF16)
    vtok = sb("vtok", [128, S], F32)
    pbuf = sb("pbuf", [128, S], F32)
    ptb = sb("ptb", [128, S], F32)
    oth = sb("oth", [128, 512], BF16)
    otl = sb("otl", [128, 512], BF16)
    fp0 = sb("fp0", [128, 512], F32)
    fp1 = sb("fp1", [128, 512], F32)
    fpu = sb("fpu", [128, 512], BF16)
    bpc = sb("bpc", [128, 512], F32)
    st = sb("st", [128, 64], F32)
    st2 = sb("st2", [128, 64], F32)
    st3 = sb("st3", [128, 64], F32)
    idf = sb("idf_s", [128, 128], F32)
    onesb = sb("onesb_s", [16, 512], BF16)
    bc4 = sb("bc4", [128, 64], F32)
    brow = sb("brow", [1, 1024], BF16)
    brow2 = sb("brow2", [1, 1024], BF16)

    psA = nc.alloc_psum_tensor("psA", [128, 2048], F32)
    psB = nc.alloc_psum_tensor("psB", [128, 2048], F32)
    ps3A = psA.ap().rearrange("p (c n) -> p c n", c=16)
    ps3B = psB.ap().rearrange("p (c n) -> p c n", c=16)
    lnh3 = lnxh.ap().rearrange("p (c n) -> p c n", c=NC_)
    lnl3 = lnxl.ap().rearrange("p (c n) -> p c n", c=NC_)

    p = Prog(nc)

    def K(*a):
        return tuple(a)

    PSa = [K("psA", j) for j in range(4)]
    PSb = [K("psB", j) for j in range(4)]

    def dma_in(dst_ap, src_ap, writes, reads=()):
        p.op("sync", lambda e, d=dst_ap, s=src_ap: e.dma_start(d, s),
             reads=reads, writes=writes, dma=True)

    def dma_out(dst_ap, src_ap, reads, writes=()):
        p.op("gpsimd", lambda e, d=dst_ap, s=src_ap: e.dma_start(d, s),
             reads=reads, writes=writes, dma=True)

    def mm(out_ap, lhsT_ap, rhs_ap, start, stop, reads, writes):
        p.op("tensor", lambda e, o=out_ap, l=lhsT_ap, r=rhs_ap, a=start, b=stop:
             e.matmul(o, l, r, start=a, stop=b), reads=reads, writes=writes)

    def tr(out_ap, in_ap, reads, writes):
        p.op("tensor", lambda e, o=out_ap, i=in_ap: e.transpose(o, i, idf[:, :]),
             reads=list(reads) + [K("idf")], writes=writes)

    def vcopy(out_ap, in_ap, reads, writes):
        p.op("vector", lambda e, o=out_ap, i=in_ap: e.tensor_copy(o, i),
             reads=reads, writes=writes)

    def scopy(out_ap, in_ap, reads, writes):
        p.op("scalar", lambda e, o=out_ap, i=in_ap: e.copy(o, i),
             reads=reads, writes=writes)

    def vsub(out_ap, a_ap, b_ap, reads, writes):
        p.op("vector", lambda e, o=out_ap, a=a_ap, b=b_ap: e.tensor_sub(o, a, b),
             reads=reads, writes=writes)

    def vmul(out_ap, a_ap, b_ap, reads, writes):
        p.op("vector", lambda e, o=out_ap, a=a_ap, b=b_ap: e.tensor_mul(o, a, b),
             reads=reads, writes=writes)

    def vadd(out_ap, a_ap, b_ap, reads, writes):
        p.op("vector", lambda e, o=out_ap, a=a_ap, b=b_ap: e.tensor_add(o, a, b),
             reads=reads, writes=writes)

    def stt(out_ap, in0_ap, scalar, in1_ap, op0, op1, reads, writes):
        p.op("vector", lambda e, o=out_ap, i0=in0_ap, sc=scalar, i1=in1_ap, a=op0, b=op1:
             e.scalar_tensor_tensor(o, i0, sc, i1, a, b), reads=reads, writes=writes)

    def ts(out_ap, in_ap, s1, s2, op0, op1, reads, writes, accum=None):
        if accum is None:
            p.op("vector", lambda e, o=out_ap, i=in_ap, a=s1, b=s2, x=op0, y=op1:
                 e.tensor_scalar(o, i, a, b, x, y), reads=reads, writes=writes)
        else:
            p.op("vector", lambda e, o=out_ap, i=in_ap, a=s1, b=s2, x=op0, y=op1, ac=accum:
                 e.tensor_scalar(o, i, a, b, x, y, accum_out=ac), reads=reads, writes=writes)

    def act(out_ap, in_ap, func, bias, scale, reads, writes, accum=None):
        if accum is None:
            p.op("scalar", lambda e, o=out_ap, i=in_ap, f=func, b=bias, s=scale:
                 e.activation(o, i, f, bias=b, scale=s), reads=reads, writes=writes)
        else:
            p.op("scalar", lambda e, o=out_ap, i=in_ap, f=func, b=bias, s=scale, ac=accum:
                 e.activation(o, i, f, bias=b, scale=s, accum_out=ac),
                 reads=reads, writes=writes)

    def reduce_absmax(out_ap, in_ap, reads, writes):
        p.op("vector", lambda e, o=out_ap, i=in_ap: e.reduce_max(
             o, i, axis=AX.X, apply_absolute_value=True), reads=reads, writes=writes)

    def recip(out_ap, in_ap, reads, writes):
        p.op("vector", lambda e, o=out_ap, i=in_ap: e.reciprocal(o, i),
             reads=reads, writes=writes)

    VELT = float(2 ** 29 + 1)

    def r32_guard(A, ka, B, kb, C, kc):
        p.op("scalar", lambda e, o=B, i=A: e.activation(
             o, i, AF.Identity, bias=0.0, scale=VELT), reads=[ka], writes=[kb])
        p.op("vector", lambda e, o=C, x=B, y=A: e.tensor_sub(o, x, y),
             reads=[kb, ka], writes=[kc])
        p.op("vector", lambda e, o=B, x=B, y=C: e.tensor_sub(o, x, y),
             reads=[kb, kc], writes=[kb])
        p.op("vector", lambda e, o=B, x=B, y=A: e.tensor_sub(o, x, y),
             reads=[kb, ka], writes=[kb])
        p.op("scalar", lambda e, o=C, i=A: e.activation(
             o, i, AF.Abs, bias=0.0, scale=float(2.0 ** -23)),
             reads=[ka], writes=[kc])
        p.op("vector", lambda e, o=B, x=B, y=C: e.tensor_tensor(o, x, y, ALU.min),
             reads=[kb, kc], writes=[kb])
        ts(C, C, -1.0, None, ALU.mult, ALU.bypass, reads=[kc], writes=[kc])
        p.op("vector", lambda e, o=B, x=B, y=C: e.tensor_tensor(o, x, y, ALU.max),
             reads=[kb, kc], writes=[kb])
        p.op("vector", lambda e, o=A, x=A, y=B: e.tensor_add(o, x, y),
             reads=[ka, kb], writes=[ka])

    dma_in(idf[:, :], idf_d[:, :], writes=[K("idf")])
    dma_in(onesb[:, :], ones_d[:, :], writes=[K("onesb")])
    for bi, nm in enumerate(("bq", "bk", "bv", "bo")):
        bl = bpc[0:16, 0:128]
        dma_in(bl, b_d[nm].rearrange("(a b) -> a b", b=128), writes=[K("bpc")])
        p.op("tensor", lambda e, o=psA[:, 0:16], i=bl: e.transpose(
             o, i, idf[0:16, 0:16]), reads=[K("bpc"), K("idf")], writes=[PSa[0]])
        vcopy(bc4[:, 16 * bi:16 * bi + 16], psA[:, 0:16],
              reads=[PSa[0]], writes=[K("bc4", bi)])
    blf = bpc[0:16, 0:128]
    dma_in(blf, b_d["bf"].rearrange("(a b) -> a b", b=128), writes=[K("bpc")])
    vcopy(fpu[0:16, 0:128], blf, reads=[K("bpc")], writes=[K("fpu")])
    p.op("vector", lambda e, o=fpu[0:16, 128:256], a=blf, b=fpu[0:16, 0:128]:
         e.tensor_sub(o, a, b), reads=[K("bpc"), K("fpu")], writes=[K("fpu")])
    dma_out(sc_b[:, :], fpu[0:16, 0:256], reads=[K("fpu")], writes=[K("sc_b")])

    def bias_col(bi, c):
        return bc4[:, 16 * bi + c:16 * bi + c + 1]

    p.op("gpsimd", lambda e: e.memset(bpc[:, 0:128], 0.0), writes=[K("bpc")])

    _slot = [0]

    def ln_stats(src_buf, read_keys):
        sl = _slot[0] % 2
        _slot[0] += 1
        bno = st2[:, 24 * sl:24 * sl + 24]
        for i in range(4):
            p.op("vector", lambda e, o=st2[:, 24 * sl + 6 * i:24 * sl + 6 * i + 6],
                 s=src_buf[:, 512 * i:512 * (i + 1)]: e.bn_stats(o, s),
                 reads=read_keys, writes=[K("bn", sl, i)])
        agg = st[:, 8 * sl:8 * sl + 2]
        p.op("vector", lambda e, o=agg, i=bno: e.bn_aggr(o, i),
             reads=[K("bn", sl, i) for i in range(4)], writes=[K("lnst", sl, "mv")])
        mean = agg[:, 0:1]
        var = agg[:, 1:2]
        t_ = st[:, 8 * sl + 2:8 * sl + 3]
        ts(t_, var, 1.0, EPS, ALU.mult, ALU.add,
           reads=[K("lnst", sl, "mv")], writes=[K("lnst", sl, "t")])
        lnt = st[:, 8 * sl + 3:8 * sl + 4]
        act(lnt, t_, AF.Ln, 0.0, 1.0, reads=[K("lnst", sl, "t")],
            writes=[K("lnst", sl, "lnt")])
        r0 = st[:, 8 * sl + 4:8 * sl + 5]
        act(r0, lnt, AF.Exp, 0.0, -0.5, reads=[K("lnst", sl, "lnt")],
            writes=[K("lnst", sl, "r0")])
        u_ = st[:, 8 * sl + 5:8 * sl + 6]
        vmul(u_, r0, r0, reads=[K("lnst", sl, "r0")], writes=[K("lnst", sl, "u")])
        vmul(u_, u_, t_, reads=[K("lnst", sl, "u"), K("lnst", sl, "t")],
             writes=[K("lnst", sl, "u")])
        ts(u_, u_, -0.5, 1.5, ALU.mult, ALU.add,
           reads=[K("lnst", sl, "u")], writes=[K("lnst", sl, "u")])
        r1 = st[:, 8 * sl + 6:8 * sl + 7]
        vmul(r1, r0, u_, reads=[K("lnst", sl, "r0"), K("lnst", sl, "u")],
             writes=[K("lnst", sl, "r1")])
        nb = st2[:, 48 + sl:49 + sl]
        vmul(nb, mean, r1, reads=[K("lnst", sl, "mv"), K("lnst", sl, "r1")],
             writes=[K("lnst", sl, "nb")])
        ts(nb, nb, -1.0, None, ALU.mult, ALU.bypass,
           reads=[K("lnst", sl, "nb")], writes=[K("lnst", sl, "nb")])
        return r1, nb, [K("lnst", sl, "r1"), K("lnst", sl, "nb")]

    def transpose_split(src_buf, src_keys, col, pp, dsth3=lnh3, dstl3=lnl3,
                        hname="lnxh", lname="lnxl"):
        ps = psA if pp == 0 else psB
        ps3 = ps3A if pp == 0 else ps3B
        pk = PSa if pp == 0 else PSb
        for c in range(16):
            tr(ps[:, 128 * c:128 * (c + 1)], src_buf[:, 128 * c:128 * (c + 1)],
               reads=src_keys, writes=[pk[c // 4]])
        for b in range(4):
            dh = dsth3[:, 4 * b:4 * b + 4, 128 * col:128 * (col + 1)]
            dl = dstl3[:, 4 * b:4 * b + 4, 128 * col:128 * (col + 1)]
            sp = ps3[:, 4 * b:4 * b + 4, :]
            hk = [K(hname, 4 * b + j, col) for j in range(4)]
            lk = [K(lname, 4 * b + j, col) for j in range(4)]
            p.op("scalar", lambda e, o=dh, i=sp: e.copy(o, i),
                 reads=[pk[b]], writes=hk)
            p.op("vector", lambda e, o=dl, i=sp, hh=dh: e.scalar_tensor_tensor(
                 o, i, 1.0, hh, ALU.mult, ALU.subtract),
                 reads=[pk[b]] + hk, writes=lk)

    PBUF_ALL = [K("pbuf")] + [K("pbuf", j) for j in range(4)]
    for tt in range(NT):
        dma_in(pbuf[:, :], xin[128 * tt:128 * (tt + 1), :], writes=PBUF_ALL)
        s_ap, b_ap, skeys = ln_stats(pbuf, [K("pbuf")])
        act(ptb[:, :], pbuf[:, :], AF.Identity, b_ap, s_ap,
            reads=[K("pbuf")] + skeys, writes=[K("ptb")])
        transpose_split(ptb, [K("ptb")], tt, tt % 2)

    LNX_ALL = [K("lnxh", c, t) for c in range(NC_) for t in range(NT)] + \
              [K("lnxl", c, t) for c in range(NC_) for t in range(NT)]

    def lnx_keys(g):
        ks = []
        for c in range(NC_):
            for t_ in range(4 * g, 4 * g + 4):
                ks.append(K("lnxh", c, t_))
                ks.append(K("lnxl", c, t_))
        return ks

    LNXG = [lnx_keys(g) for g in range(NG)]

    def wprep(dram, c, idx_h, idx_l, pp):
        dma_in(wrow[:, 0:1024], dram[128 * c:128 * (c + 1), 0:1024],
               writes=[K("wrow", 0)])
        dma_in(wrow[:, 1024:2048], dram[128 * c:128 * (c + 1), 1024:2048],
               writes=[K("wrow", 1)])
        ps = psA if pp == 0 else psB
        pk = PSa if pp == 0 else PSb
        for j in range(16):
            tr(ps[:, 128 * j:128 * (j + 1)], wrow[:, 128 * j:128 * (j + 1)],
               reads=[K("wrow", j // 8)], writes=[pk[j // 4]])
        dh, dl = wqt[idx_h], wqt[idx_l]
        for pc in range(4):
            sl = slice(512 * pc, 512 * (pc + 1))
            scopy(dh[:, sl], ps[:, sl], reads=[pk[pc]],
                  writes=[K("wqt", idx_h, pc)])
            p.op("vector", lambda e, o=dl[:, sl], i=ps[:, sl], hh=dh[:, sl]:
                 e.scalar_tensor_tensor(o, i, 1.0, hh, ALU.mult, ALU.subtract),
                 reads=[pk[pc], K("wqt", idx_h, pc)],
                 writes=[K("wqt", idx_l, pc)])

    def proj_tile(dst, dk, g, idx_h, idx_l):
        rk = LNXG[g]
        for c in range(NC_):
            lh = lnh3[:, c, 512 * g:512 * (g + 1)]
            ll = lnl3[:, c, 512 * g:512 * (g + 1)]
            whs = wqt[idx_h][:, 128 * c:128 * (c + 1)]
            wls = wqt[idx_l][:, 128 * c:128 * (c + 1)]
            kh_ = K("wqt", idx_h, c // 4)
            kl_ = K("wqt", idx_l, c // 4)
            mm(dst, whs, lh, c == 0, False, reads=[kh_] + (rk if c == 0 else []),
               writes=[dk])
            mm(dst, whs, ll, False, False, reads=[kh_], writes=[dk])
            mm(dst, wls, lh, False, c == NC_ - 1, reads=[kl_], writes=[dk])

    def make_w_pass1(h):
        ems = []
        wam = st3[:, 48:49]
        for pc in range(8):
            def em(pc=pc, h=h):
                land = bpc[:, 256:512]
                dma_in(land, w_d["wf"][128 * h:128 * (h + 1), 256 * pc:256 * (pc + 1)],
                       writes=[K("bpcw")])
                if pc == 0:
                    reduce_absmax(wam, land, reads=[K("bpcw")], writes=[K("wam")])
                else:
                    amp = st3[:, 53:54]
                    reduce_absmax(amp, land, reads=[K("bpcw")], writes=[K("amp")])
                    p.op("vector", lambda e, o=wam, a=wam, b=amp:
                         e.tensor_tensor(o, a, b, ALU.max),
                         reads=[K("wam"), K("amp")], writes=[K("wam")])
                if pc == 7:
                    wscol = st3[:, h:h + 1]
                    ts(wscol, wam, 1e-8, 1.0 / 127.0, ALU.max, ALU.mult,
                       reads=[K("wam")], writes=[K("ws", h)])
                    wrc = st3[:, 49:50]
                    recip(wrc, wscol, reads=[K("ws", h)], writes=[K("wrc")])
                    r32_guard(wscol, K("ws", h), st3[:, 51:52], K("sc51"),
                              st3[:, 52:53], K("sc52"))
            ems.append(em)
        return ems

    def make_w_pass2(h):
        ems = []
        wrc = st3[:, 49:50]
        for pc in range(8):
            def em(pc=pc, h=h):
                land = bpc[:, 256:512]
                A, B = fp1[:, 0:256], fp1[:, 256:512]
                U = fpu[:, 0:256]
                dma_in(land, w_d["wf"][128 * h:128 * (h + 1), 256 * pc:256 * (pc + 1)],
                       writes=[K("bpcw")])
                ts(A, land, wrc, 128.5, ALU.mult, ALU.add,
                   reads=[K("bpcw"), K("wrc")], writes=[K("fp1")])
                scopy(B, A, reads=[K("fp1")], writes=[K("fp1")])
                for hv in PEEL:
                    ts(U, B, hv, hv, ALU.is_ge, ALU.mult,
                       reads=[K("fp1")], writes=[K("fpu")])
                    vsub(B, B, U, reads=[K("fp1"), K("fpu")], writes=[K("fp1")])
                p.op("vector", lambda e, o=B, i=A, i1=B:
                     e.scalar_tensor_tensor(o, i, -128.0, i1, ALU.add, ALU.subtract),
                     reads=[K("fp1")], writes=[K("fp1")])
                dma_out(sc_wq[128 * h:128 * (h + 1), 256 * pc:256 * (pc + 1)], B,
                        reads=[K("fp1")], writes=[K("sc_wq", h, pc // 2)])
            ems.append(em)
        return ems

    qcol = [0]
    for h in range(NH):
        pair_a = (0, 1) if h % 2 == 0 else (2, 3)
        pair_b = (2, 3) if h % 2 == 0 else (0, 1)
        _wp1 = make_w_pass1(h)
        _wp2 = make_w_pass2(h)
        wprep(w_d["wv"], h, pair_a[0], pair_a[1], 0)
        for g in range(NG):
            ps = psA if g % 2 == 0 else psB
            pk = PSa if g % 2 == 0 else PSb
            dst = ps[:, 0:512]
            proj_tile(dst, pk[0], g, pair_a[0], pair_a[1])
            stg, stgk = (fp0, K("fp0")) if g % 2 == 0 else (fp1, K("fp1"))
            act(stg[:, :], dst, AF.Identity, bias_col(2, h), 1.0,
                reads=[pk[0], K("bc4", 2)], writes=[stgk])
            ps2 = psB if g % 2 == 0 else psA
            pk2 = PSb if g % 2 == 0 else PSa
            for j in range(4):
                tr(ps2[:, 512 + 128 * j:512 + 128 * (j + 1)], stg[:, 128 * j:128 * (j + 1)],
                   reads=[stgk], writes=[pk2[1]])
            vcopy(vtok[:, 512 * g:512 * (g + 1)], ps2[:, 512:1024],
                  reads=[pk2[1]], writes=[K("vtok", g)])
            _wp1[2 * g]()
            _wp1[2 * g + 1]()
        wprep(w_d["wq"], h, pair_b[0], pair_b[1], 0)
        for g in range(NG):
            ps = psA if g % 2 == 0 else psB
            pk = PSa if g % 2 == 0 else PSb
            dst = ps[:, 0:512]
            proj_tile(dst, pk[0], g, pair_b[0], pair_b[1])
            act(qh[:, 512 * g:512 * (g + 1)], dst, AF.Identity, bias_col(0, h), 1.0,
                reads=[pk[0], K("bc4", 0)], writes=[K("qh", g)])
            p.op("vector", lambda e, o=ql[:, 512 * g:512 * (g + 1)], i=dst,
                 hh=qh[:, 512 * g:512 * (g + 1)], bb=bias_col(0, h):
                 e.scalar_tensor_tensor(o, i, bb, hh, ALU.add, ALU.subtract),
                 reads=[pk[0], K("qh", g), K("bc4", 0)], writes=[K("ql", g)])
            _wp2[g]()
        wprep(w_d["wk"], h, pair_a[0], pair_a[1], 1)
        for g in range(NG):
            ps = psA if g % 2 == 0 else psB
            pk = PSa if g % 2 == 0 else PSb
            dst = ps[:, 0:512]
            proj_tile(dst, pk[0], g, pair_a[0], pair_a[1])
            act(kh[:, 512 * g:512 * (g + 1)], dst, AF.Identity, bias_col(1, h), 1.0,
                reads=[pk[0], K("bc4", 1)], writes=[K("kh", g)])
            p.op("vector", lambda e, o=kl[:, 512 * g:512 * (g + 1)], i=dst,
                 hh=kh[:, 512 * g:512 * (g + 1)], bb=bias_col(1, h):
                 e.scalar_tensor_tensor(o, i, bb, hh, ALU.add, ALU.subtract),
                 reads=[pk[0], K("kh", g), K("bc4", 1)], writes=[K("kl", g)])
            _wp2[4 + g]()

        for qt in range(NT):
            ps = psA if qt % 2 == 0 else psB
            pk = PSa if qt % 2 == 0 else PSb
            lq_h = qh[:, 128 * qt:128 * (qt + 1)]
            lq_l = ql[:, 128 * qt:128 * (qt + 1)]
            qkeys = [K("qh", qt // 4), K("ql", qt // 4)]
            c0 = qcol[0] % 8
            qcol[0] += 1
            nm = st[:, 24 + c0:25 + c0]
            rc = st[:, 40 + c0:41 + c0]
            nmj = [st2[:, 50 + jj:51 + jj] for jj in range(4)]
            smj = [st2[:, 54 + jj:55 + jj] for jj in range(4)]
            for j in range(NG):
                dst = ps[:, 512 * j:512 * (j + 1)]
                rh = kh[:, 512 * j:512 * (j + 1)]
                rl = kl[:, 512 * j:512 * (j + 1)]
                kkeys = [K("kh", j), K("kl", j)]
                mm(dst, lq_h, rh, True, False, reads=qkeys + kkeys, writes=[pk[j]])
                mm(dst, lq_h, rl, False, False, reads=[], writes=[pk[j]])
                mm(dst, lq_l, rh, False, True, reads=[], writes=[pk[j]])
                ts(pbuf[:, 512 * j:512 * (j + 1)], dst, -SCALE, 3.0e38,
                   ALU.mult, ALU.min, reads=[pk[j]],
                   writes=[K("pbuf", j), K("nmj", j)], accum=nmj[j])
            p.op("vector", lambda e, o=nm, a=nmj[0], b=nmj[1]: e.tensor_tensor(
                 o, a, b, ALU.min), reads=[K("nmj", 0), K("nmj", 1)],
                 writes=[K("nm", c0)])
            p.op("vector", lambda e, o=nm, a=nm, b=nmj[2]: e.tensor_tensor(
                 o, a, b, ALU.min), reads=[K("nm", c0), K("nmj", 2)],
                 writes=[K("nm", c0)])
            p.op("vector", lambda e, o=nm, a=nm, b=nmj[3]: e.tensor_tensor(
                 o, a, b, ALU.min), reads=[K("nm", c0), K("nmj", 3)],
                 writes=[K("nm", c0)])
            for j in range(2):
                act(pbuf[:, 1024 * j:1024 * (j + 1)], ps[:, 1024 * j:1024 * (j + 1)],
                    AF.Exp, nm, SCALE,
                    reads=[pk[2 * j], pk[2 * j + 1], K("nm", c0)],
                    writes=[K("pbuf", 2 * j), K("pbuf", 2 * j + 1), K("smj", j)],
                    accum=smj[j])
            p.op("vector", lambda e, o=rc, a=smj[0], b=smj[1]: e.tensor_add(o, a, b),
                 reads=[K("smj", 0), K("smj", 1)], writes=[K("rc", c0)])
            recip(rc, rc, reads=[K("rc", c0)], writes=[K("rc", c0)])
            for j in range(16):
                tr(ps[:, 128 * j:128 * (j + 1)], pbuf[:, 128 * j:128 * (j + 1)],
                   reads=[K("pbuf", j // 4)], writes=[pk[j // 4]])
            scopy(ptb[:, :], ps[:, :], reads=[pk[0], pk[1], pk[2], pk[3]],
                  writes=[K("ptb")])
            vcopy(bpc[:, 0:1], rc, reads=[K("rc", c0)], writes=[K("bpc")])
            p.op("tensor", lambda e, o=ps[:, 256:384], i=bpc[:, 0:128]:
                 e.transpose(o, i, idf[:, :]),
                 reads=[K("bpc"), K("idf")], writes=[pk[0]])
            vcopy(brow2[0:1, 0:128], ps[0:1, 256:384], reads=[pk[0]],
                  writes=[K("brow2")])
            p.op("vector", lambda e, o=brow2[0:1, 128:256], i=ps[0:1, 256:384],
                 hh=brow2[0:1, 0:128]: e.scalar_tensor_tensor(
                 o, i, 1.0, hh, ALU.mult, ALU.subtract),
                 reads=[pk[0], K("brow2")], writes=[K("brow2")])
            mm(ps[:, 128:256], onesb[0:1, 0:128], brow2[0:1, 0:128], True, False,
               reads=[K("onesb"), K("brow2")], writes=[pk[0]])
            mm(ps[:, 128:256], onesb[0:1, 0:128], brow2[0:1, 128:256], False, True,
               reads=[K("brow2")], writes=[pk[0]])
            vcopy(bpc[:, 128:256], ps[:, 128:256], reads=[pk[0]], writes=[K("bpcrc")])
            first = True
            for j in range(NC_):
                mm(ps[:, 0:128], vtok[:, 128 * j:128 * (j + 1)],
                   ptb[:, 128 * j:128 * (j + 1)], first, j == NC_ - 1,
                   reads=([K("vtok", jj) for jj in range(4)] + [K("ptb")]) if first else [],
                   writes=[pk[0]])
                first = False
            q4 = qt % 4
            p.op("vector", lambda e, o=fp0[:, 0:128], a=ps[:, 0:128],
                 b=bpc[:, 128:256]: e.tensor_mul(o, a, b),
                 reads=[pk[0], K("bpcrc")], writes=[K("fp0")])
            vcopy(oth[:, 128 * q4:128 * (q4 + 1)], fp0[:, 0:128],
                  reads=[K("fp0")], writes=[K("oth", q4)])
            p.op("vector", lambda e, o=otl[:, 128 * q4:128 * (q4 + 1)],
                 i=fp0[:, 0:128], hh=oth[:, 128 * q4:128 * (q4 + 1)]:
                 e.scalar_tensor_tensor(o, i, 1.0, hh, ALU.mult, ALU.subtract),
                 reads=[K("fp0"), K("oth", q4)], writes=[K("otl", q4)])
            if q4 == 3:
                gq = qt // 4
                blk = (4 * gq + h // 4) * 128
                cs = 512 * (h % 4)
                dma_out(sc_oh[blk:blk + 128, cs:cs + 512],
                        oth[:, :], reads=[K("oth", i) for i in range(4)],
                        writes=[K("sc_oh", gq, h)])
                dma_out(sc_ol[blk:blk + 128, cs:cs + 512],
                        otl[:, :], reads=[K("otl", i) for i in range(4)],
                        writes=[K("sc_ol", gq, h)])

    for oc in range(NC_):
        dma_in(wrow[:, :], w_d["wo"][128 * oc:128 * (oc + 1), :], writes=[K("wrow"), K("wrow", 0), K("wrow", 1)])
        transpose_split(wrow, [K("wrow")], oc, oc % 2)

    ya = [pbuf, ptb, vtok, wrow]
    yak = ["pbuf", "ptb", "vtok", "wrow"]
    ogb = [qh, ql, kh, kl]
    ogbk = ["qh", "ql", "kh", "kl"]

    def ya_keys(t_):
        if yak[t_] == "pbuf":
            return [K("pbuf")] + [K("pbuf", j) for j in range(4)]
        if yak[t_] == "wrow":
            return [K("wrow"), K("wrow", 0), K("wrow", 1)]
        if yak[t_] == "ptb":
            return [K("ptb")]
        return [K("vtok", i) for i in range(4)]

    for g in range(NG):
        for b in range(4):
            blk = (4 * g + b) * 128
            dma_in(ogb[b][:, :], sc_oh[blk:blk + 128, :],
                   writes=[K(ogbk[b], q) for q in range(4)],
                   reads=[K("sc_oh", g, 4 * b + q) for q in range(4)])
            dma_in(wqt[b][:, :], sc_ol[blk:blk + 128, :],
                   writes=[K("wqt", b, q) for q in range(4)],
                   reads=[K("sc_ol", g, 4 * b + q) for q in range(4)])

        def oglo_key(c):
            return K("wqt", c // 4, c % 4)

        for ocg in range(4):
            for oc2 in range(4):
                oc = 4 * ocg + oc2
                dst = psA[:, 512 * (oc2 % 2):512 * (oc2 % 2) + 512]
                dk = PSa[oc2 % 2]
                wo_keys = [K("lnxh", c_, oc) for c_ in range(NC_)] + \
                          [K("lnxl", c_, oc) for c_ in range(NC_)]
                for c in range(NC_):
                    ah = ogb[c // 4][:, 512 * (c % 4):512 * (c % 4 + 1)]
                    al_ = wqt[c // 4][:, 512 * (c % 4):512 * (c % 4 + 1)]
                    akeys = [K(ogbk[c // 4], c % 4), oglo_key(c)]
                    wh_ = lnh3[:, c, 128 * oc:128 * (oc + 1)]
                    wl_ = lnl3[:, c, 128 * oc:128 * (oc + 1)]
                    mm(dst, wh_, ah, c == 0, False,
                       reads=(wo_keys if c == 0 else []) + akeys, writes=[dk])
                    mm(dst, wh_, al_, False, False, reads=[], writes=[dk])
                    mm(dst, wl_, ah, False, c == NC_ - 1, reads=[], writes=[dk])
                stg, stgk = (fp0, K("fp0")) if oc2 % 2 == 0 else (fp1, K("fp1"))
                act(stg[:, :], dst, AF.Identity, bias_col(3, oc), 1.0,
                    reads=[dk, K("bc4", 3)], writes=[stgk])
                for t_ in range(4):
                    tr(psB[:, 512 * t_ + 128 * oc2:512 * t_ + 128 * (oc2 + 1)],
                       stg[:, 128 * t_:128 * (t_ + 1)], reads=[stgk], writes=[PSb[t_]])
            for t_ in range(4):
                vcopy(ya[t_][:, 512 * ocg:512 * (ocg + 1)],
                      psB[:, 512 * t_:512 * (t_ + 1)],
                      reads=[PSb[t_]], writes=ya_keys(t_))
        for t_ in range(4):
            tt = 4 * g + t_
            yk = ya_keys(t_)
            s_ap, b_ap, skeys = ln_stats(ya[t_], yk)
            for h2 in range(2):
                sl = slice(1024 * h2, 1024 * (h2 + 1))
                bks = [PSb[2 * h2], PSb[2 * h2 + 1]]
                act(psB[:, sl], ya[t_][:, sl], AF.Identity, b_ap, s_ap,
                    reads=yk + skeys, writes=bks)
                dma_in(ya[t_][:, sl], xin[128 * tt:128 * (tt + 1), sl],
                       writes=yk, reads=bks)
                p.op("vector", lambda e, o=ya[t_][:, sl], a=ya[t_][:, sl],
                     b=psB[:, sl]: e.tensor_add(o, a, b),
                     reads=yk + bks, writes=yk)
                dma_out(sc_r2[128 * tt:128 * (tt + 1), sl], ya[t_][:, sl],
                        reads=yk, writes=[K("sc_r2", tt, 2 * h2), K("sc_r2", tt, 2 * h2 + 1)])

    def quant_peel(src_buf, src_keys, rcol, rkey, col, hname_lname, pp,
                   eng="vector", bufs=None):
        d3 = lnh3 if hname_lname == "lnxh" else lnl3
        ps = psA if pp == 0 else psB
        ps_ = ps3A if pp == 0 else ps3B
        pk = PSa if pp == 0 else PSb
        if bufs is None:
            A, kA, B, kB, U, kU, W_ = fp0, K("fp0"), fp1, K("fp1"), fpu, K("fpu"), 512
        else:
            A, kA, B, kB, U, kU, W_ = bufs
        npc = 2048 // W_

        def tseng(out_ap, in_ap, s1, s2, op0, op1, reads, writes):
            p.op(eng, lambda e, o=out_ap, i=in_ap, a=s1, b=s2, x=op0, y=op1:
                 e.tensor_scalar(o, i, a, b, x, y), reads=reads, writes=writes)

        for pc in range(npc):
            sl = src_buf[:, W_ * pc:W_ * (pc + 1)]
            Asl, Bsl, Usl = A[:, 0:W_], B[:, 0:W_], U[:, 0:W_]
            tseng(Asl, sl, rcol, 128.5, ALU.mult, ALU.add,
                  reads=list(src_keys) + [rkey], writes=[kA])
            p.op("scalar" if eng == "vector" else eng,
                 lambda e, o=Bsl, i=Asl: e.copy(o, i) if hasattr(e, "copy")
                 else e.tensor_copy(o, i), reads=[kA], writes=[kB])
            for hv in PEEL:
                tseng(Usl, Bsl, hv, hv, ALU.is_ge, ALU.mult,
                      reads=[kB], writes=[kU])
                p.op(eng, lambda e, o=Bsl, a=Bsl, b=Usl: e.tensor_sub(o, a, b),
                     reads=[kB, kU], writes=[kB])
            if eng == "vector":
                p.op(eng, lambda e, o=Bsl, i=Asl, i1=Bsl:
                     e.scalar_tensor_tensor(o, i, -128.0, i1, ALU.add, ALU.subtract),
                     reads=[kA, kB], writes=[kB])
            else:
                tseng(Asl, Asl, -128.0, None, ALU.add, ALU.bypass,
                      reads=[kA], writes=[kA])
                p.op(eng, lambda e, o=Bsl, a=Asl, b=Bsl: e.tensor_sub(o, a, b),
                     reads=[kA, kB], writes=[kB])
            for j in range(W_ // 128):
                tr(ps[:, W_ * pc + 128 * j:W_ * pc + 128 * (j + 1)],
                   Bsl[:, 128 * j:128 * (j + 1)], reads=[kB],
                   writes=[pk[(W_ * pc + 128 * j) // 512]])
        for bq in range(4):
            dh = d3[:, 4 * bq:4 * bq + 4, 128 * col:128 * (col + 1)]
            hk = [K(hname_lname, 4 * bq + j, col) for j in range(4)]
            p.op("scalar", lambda e, o=dh, i=ps_[:, 4 * bq:4 * bq + 4, :]:
                 e.copy(o, i), reads=[pk[bq]], writes=hk)

    for oc in range(NC_):
        dma_in(wrow[:, :], sc_wq[128 * oc:128 * (oc + 1), :],
               writes=[K("wrow"), K("wrow", 0), K("wrow", 1)],
               reads=[K("sc_wq", oc, q) for q in range(4)])
        for j in range(16):
            tr(psA[:, 128 * j:128 * (j + 1)], wrow[:, 128 * j:128 * (j + 1)],
               reads=[K("wrow"), K("wrow", 0), K("wrow", 1)],
               writes=[PSa[j // 4]])
        for bq in range(4):
            dh = lnl3[:, 4 * bq:4 * bq + 4, 128 * oc:128 * (oc + 1)]
            hk = [K("lnxl", 4 * bq + j, oc) for j in range(4)]
            p.op("scalar", lambda e, o=dh, i=ps3A[:, 4 * bq:4 * bq + 4, :]:
                 e.copy(o, i), reads=[PSa[bq]], writes=hk)

    p.op("gpsimd", lambda e: e.memset(fp0[:, 0:128], 0.0), writes=[K("fp0")])
    vcopy(fp0[:, 0:16], st3[:, 0:16], reads=[K("ws", oc_) for oc_ in range(NC_)],
          writes=[K("fp0")])
    p.op("tensor", lambda e: e.transpose(psA[:, 0:128], fp0[:, 0:128], idf[:, :]),
         reads=[K("fp0"), K("idf")], writes=[PSa[0]])
    vcopy(fpu[0:16, 0:128], psA[0:16, 0:128], reads=[PSa[0]], writes=[K("fpu")])
    p.op("vector", lambda e: e.scalar_tensor_tensor(
         fpu[0:16, 128:256], psA[0:16, 0:128], 1.0, fpu[0:16, 0:128],
         ALU.mult, ALU.subtract), reads=[PSa[0], K("fpu")], writes=[K("fpu")])
    dma_out(sc_ws[:, :], fpu[0:16, 0:256], reads=[K("fpu")], writes=[K("sc_ws")])

    for tt in range(NT):
        dma_in(pbuf[:, :], sc_r2[128 * tt:128 * (tt + 1), :], writes=PBUF_ALL,
               reads=[K("sc_r2", tt, pc_) for pc_ in range(4)])
        s_ap, b_ap, skeys = ln_stats(pbuf, [K("pbuf")])
        act(ptb[:, :], pbuf[:, :], AF.Identity, b_ap, s_ap,
            reads=[K("pbuf")] + skeys, writes=[K("ptb")])
        xam = st3[:, 50:51]
        reduce_absmax(xam, ptb[:, :], reads=[K("ptb")], writes=[K("xam")])
        xscol = st3[:, 16 + tt:17 + tt]
        ts(xscol, xam, 1.0 / 127.0, 1e-12, ALU.mult, ALU.max,
           reads=[K("xam")], writes=[K("xs", tt)])
        xrcol = st3[:, 32 + tt:33 + tt]
        recip(xrcol, xscol, reads=[K("xs", tt)], writes=[K("xr", tt)])
        quant_peel(ptb, [K("ptb")], xrcol, K("xr", tt), tt, "lnxh", 1)
        xs32col = st[:, 48 + tt:49 + tt]
        vcopy(xs32col, xscol, reads=[K("xs", tt)], writes=[K("xs32", tt)])
        r32_guard(xs32col, K("xs32", tt), st3[:, 53:54], K("sc53"),
                  st3[:, 54:55], K("sc54"))

    for g in range(NG):
        for ocg in range(4):
            dma_in(brow[0:1, :], sc_b[4 * ocg:4 * ocg + 4, :].rearrange("a b -> (a b)").rearrange("(a b) -> a b", a=1),
                   writes=[K("brow")], reads=[K("sc_b")])
            for oc2 in range(4):
                dstb = psA[:, 1536 + 128 * oc2:1536 + 128 * (oc2 + 1)]
                mm(dstb, onesb[0:1, 0:128], brow[0:1, 256 * oc2:256 * oc2 + 128],
                   True, False, reads=[K("onesb"), K("brow")], writes=[PSa[3]])
                mm(dstb, onesb[0:1, 0:128], brow[0:1, 256 * oc2 + 128:256 * oc2 + 256],
                   False, True, reads=[K("onesb"), K("brow")], writes=[PSa[3]])
            vcopy(bpc[:, :], psA[:, 1536:2048], reads=[PSa[3]],
                  writes=[K("bpc"), K("bpcrc"), K("bpcw")])
            dma_in(brow2[0:1, :], sc_ws[4 * ocg:4 * ocg + 4, :].rearrange("a b -> (a b)").rearrange("(a b) -> a b", a=1),
                   writes=[K("brow2")], reads=[K("sc_ws")])
            for oc2 in range(4):
                wdst = psA[:, 1024 + 128 * oc2:1024 + 128 * (oc2 + 1)]
                mm(wdst, onesb[0:1, 0:128], brow2[0:1, 256 * oc2:256 * oc2 + 128],
                   True, False, reads=[K("onesb"), K("brow2")], writes=[PSa[2]])
                mm(wdst, onesb[0:1, 0:128], brow2[0:1, 256 * oc2 + 128:256 * oc2 + 256],
                   False, True, reads=[K("onesb"), K("brow2")], writes=[PSa[2]])
            for oc2 in range(4):
                oc = 4 * ocg + oc2
                dst = psA[:, 512 * (oc2 % 2):512 * (oc2 % 2) + 512]
                dk = PSa[oc2 % 2]
                first = True
                for c in range(NC_):
                    wslice = lnl3[:, c, 128 * oc:128 * (oc + 1)]
                    xslice = lnh3[:, c, 512 * g:512 * (g + 1)]
                    rks = [K("lnxl", c, oc)] + \
                          [K("lnxh", c, 4 * g + i) for i in range(4)]
                    mm(dst, wslice, xslice, first, c == NC_ - 1,
                       reads=rks, writes=[dk])
                    first = False
                stg, stgk = (fp0, K("fp0")) if oc2 % 2 == 0 else (fp1, K("fp1"))
                scopy(stg[:, :], dst, reads=[dk], writes=[stgk])
                for t_ in range(4):
                    tr(psB[:, 512 * t_ + 128 * oc2:512 * t_ + 128 * (oc2 + 1)],
                       stg[:, 128 * t_:128 * (t_ + 1)], reads=[stgk], writes=[PSb[t_]])
            for t_ in range(4):
                tt = 4 * g + t_
                xs32c = st[:, 48 + tt:49 + tt]
                yasl = ya[t_][:, 512 * ocg:512 * (ocg + 1)]
                yk = ya_keys(t_)
                p.op("scalar", lambda e, o=fp0[:, :], i=psB[:, 512 * t_:512 * (t_ + 1)],
                     sc=xs32c: e.activation(o, i, AF.Identity, bias=0.0, scale=sc),
                     reads=[PSb[t_], K("xs32", tt)], writes=[K("fp0")])
                r32_guard(fp0[:, :], K("fp0"), fp1[:, :], K("fp1"), yasl, yk[0])
                p.op("vector", lambda e, o=fp1[:, :], a=fp0[:, :],
                     b=psA[:, 1024:1536]: e.tensor_mul(o, a, b),
                     reads=[K("fp0"), PSa[2]], writes=[K("fp1")])
                r32_guard(fp1[:, :], K("fp1"), fp0[:, :], K("fp0"), yasl, yk[0])
                p.op("vector", lambda e, o=yasl, a=fp1[:, :], b=bpc[:, :]:
                     e.tensor_add(o, a, b),
                     reads=[K("fp1"), K("bpc")], writes=yk)
        for t_ in range(4):
            tt = 4 * g + t_
            yk = ya_keys(t_)
            s_ap, b_ap, skeys = ln_stats(ya[t_], yk)
            for h2 in range(2):
                sl = slice(1024 * h2, 1024 * (h2 + 1))
                bks = [PSb[2 * h2], PSb[2 * h2 + 1]]
                act(psB[:, sl], ya[t_][:, sl], AF.Identity, b_ap, s_ap,
                    reads=yk + skeys, writes=bks)
                dma_in(ya[t_][:, sl], sc_r2[128 * tt:128 * (tt + 1), sl],
                       writes=yk, reads=bks + [K("sc_r2", tt, 2 * h2), K("sc_r2", tt, 2 * h2 + 1)])
                p.op("vector", lambda e, o=ya[t_][:, sl], a=ya[t_][:, sl],
                     b=psB[:, sl]: e.tensor_add(o, a, b),
                     reads=yk + bks, writes=yk)
                dma_out(out_d[128 * tt:128 * (tt + 1), sl], ya[t_][:, sl],
                        reads=yk)

    ng_ = p.dma_idx["gpsimd"]
    p.tail_waits = {"gpsimd": [(f"dma_gpsimd_{i}",
                                ((ng_ - 1 - i) // Prog.NDMA + 1) * 16)
                               for i in range(min(Prog.NDMA, ng_))]}

    import contextlib
    stack = contextlib.ExitStack()
    handles = {}
    for s_ in p.sem_names:
        handles[s_] = stack.enter_context(nc.semaphore(s_))
    p.sems = handles
    with nc.Block() as block:
        p.emit(block)
    stack.close()
    return nc


_CACHE = {}


def kernel(L0_attn_hidden_state, L0_attn_Wq, L0_attn_bq, L0_attn_Wk, L0_attn_bk,
           L0_attn_Wv, L0_attn_bv, L0_attn_Wo, L0_attn_bo, L0_ffn_weight, L0_ffn_bias):
    if "nc" not in _CACHE:
        _CACHE["nc"] = build()
    nc = _CACHE["nc"]
    hs = np.ascontiguousarray(np.asarray(L0_attn_hidden_state, dtype=np.float32))
    common = {
        "wq": np.ascontiguousarray(L0_attn_Wq, dtype=np.float32),
        "wk": np.ascontiguousarray(L0_attn_Wk, dtype=np.float32),
        "wv": np.ascontiguousarray(L0_attn_Wv, dtype=np.float32),
        "wo": np.ascontiguousarray(L0_attn_Wo, dtype=np.float32),
        "wf": np.ascontiguousarray(L0_ffn_weight, dtype=np.float32),
        "bq": np.ascontiguousarray(L0_attn_bq, dtype=np.float32),
        "bk": np.ascontiguousarray(L0_attn_bk, dtype=np.float32),
        "bv": np.ascontiguousarray(L0_attn_bv, dtype=np.float32),
        "bo": np.ascontiguousarray(L0_attn_bo, dtype=np.float32),
        "bf": np.ascontiguousarray(L0_ffn_bias, dtype=np.float32),
        "idf": np.eye(128, dtype=np.float32),
        "onesb": np.ones((16, 512), dtype=ml_dtypes.bfloat16),
    }
    in_maps = [dict(common, x=np.ascontiguousarray(hs[b])) for b in range(8)]
    res = run_bass_kernel_spmd(nc, in_maps, core_ids=list(range(8)))
    out = np.stack([np.asarray(r["out"]) for r in res.results], axis=0)
    return out


# version 24
# speedup vs baseline: 1.1667x; 1.0799x over previous
import numpy as np
import ml_dtypes
import concourse.bass as bass
import concourse.mybir as mybir
import concourse.alu_op_type as alu_mod
from concourse.bass_utils import run_bass_kernel_spmd

ALU = alu_mod.AluOpType
F32 = mybir.dt.float32
BF16 = mybir.dt.bfloat16
AF = mybir.ActivationFunctionType
AX = mybir.AxisListType

H = 2048
NH = 16
S = 2048
NT = 16
NG = 4
NC_ = 16
SCALE = 1.0 / float(np.sqrt(128.0))
EPS = 1e-5
PEEL = (128.0, 64.0, 32.0, 16.0, 8.0, 4.0, 2.0, 1.0)

ENGINES = ("tensor", "vector", "scalar", "gpsimd", "sync")


class _Op:
    __slots__ = ("engine", "fn", "deps", "inc", "sem", "cnt", "needed", "gate")

    def __init__(self, engine, fn, inc, sem, gate):
        self.engine = engine
        self.fn = fn
        self.deps = []
        self.inc = inc
        self.sem = sem
        self.cnt = None
        self.needed = True
        self.gate = gate


class Prog:

    NDMA = 8

    def __init__(self, nc):
        self.nc = nc
        self.ops = []
        self.sems = {}
        self.sem_names = ["sem_" + e for e in ENGINES] + \
                         [f"dma_{e}_{i}" for e in ENGINES for i in range(self.NDMA)]
        self.dma_idx = {e: 0 for e in ENGINES}
        self.last_w = {}
        self.readers = {}
        self.tail_waits = {}

    def op(self, engine, fn, reads=(), writes=(), dma=False):
        idx = len(self.ops)
        gate = None
        if dma:
            qi = self.dma_idx[engine]
            self.dma_idx[engine] += 1
            slot = qi % self.NDMA
            sem = f"dma_{engine}_{slot}"
            if qi >= self.NDMA:
                gate = (sem, (qi // self.NDMA) * 16)
            o = _Op(engine, fn, 16, sem, gate)
        else:
            o = _Op(engine, fn, 1, "sem_" + engine, None)
        deps = set()
        for r in reads:
            w = self.last_w.get(r)
            if w is not None:
                deps.add(w)
        for r in writes:
            w = self.last_w.get(r)
            if w is not None:
                deps.add(w)
            for rd in self.readers.get(r, ()):
                deps.add(rd)
        o.deps = [d for d in deps
                  if not (engine == "tensor" and self.ops[d].engine == "tensor")]
        for r in reads:
            self.readers.setdefault(r, []).append(idx)
        for r in writes:
            self.last_w[r] = idx
            self.readers[r] = []
        self.ops.append(o)

    def _finalize(self):
        for o in self.ops:
            if o.engine == "tensor" and o.inc == 1:
                o.needed = False
        for o in self.ops:
            by_sem = {}
            for d in o.deps:
                s = self.ops[d].sem
                if by_sem.get(s, -1) < d:
                    by_sem[s] = d
            o.deps = sorted(by_sem.values())
            for d in o.deps:
                self.ops[d].needed = True
        counts = {s: 0 for s in self.sem_names}
        for o in self.ops:
            if o.needed:
                counts[o.sem] += o.inc
                o.cnt = counts[o.sem]
        self.final_counts = counts

    def emit(self, block):
        self._finalize()

        def emit_engine(eng_obj, name):
            wm = {}
            for o in self.ops:
                if o.engine != name:
                    continue
                waits = []
                if o.gate is not None and o.gate[1] > 0:
                    waits.append(o.gate)
                for d in o.deps:
                    p = self.ops[d]
                    if p.cnt is not None:
                        waits.append((p.sem, p.cnt))
                for (s, v) in waits:
                    if wm.get(s, 0) < v:
                        wm[s] = v
                        eng_obj.wait_ge(self.sems[s], v)
                inst = o.fn(eng_obj)
                if o.needed:
                    inst.then_inc(self.sems[o.sem], o.inc)
            for (s, v) in self.tail_waits.get(name, ()):
                eng_obj.wait_ge(self.sems[s], v)

        used = {o.engine for o in self.ops} | set(self.tail_waits)
        if "tensor" in used:
            @block.tensor
            def _(t):
                emit_engine(t, "tensor")
        if "vector" in used:
            @block.vector
            def _(v):
                emit_engine(v, "vector")
        if "scalar" in used:
            @block.scalar
            def _(s):
                emit_engine(s, "scalar")
        if "gpsimd" in used:
            @block.gpsimd
            def _(g):
                emit_engine(g, "gpsimd")
        if "sync" in used:
            @block.sync
            def _(sy):
                emit_engine(sy, "sync")


def build():
    nc = bass.Bass("TRN2", target_bir_lowering=False)

    xin = nc.dram_tensor("x", [S, H], F32, kind="ExternalInput").ap()
    w_d = {nm: nc.dram_tensor(nm, [H, H], F32, kind="ExternalInput").ap()
           for nm in ("wq", "wk", "wv", "wo", "wf")}
    b_d = {nm: nc.dram_tensor(nm, [H], F32, kind="ExternalInput").ap()
           for nm in ("bq", "bk", "bv", "bo", "bf")}
    idf_d = nc.dram_tensor("idf", [128, 128], F32, kind="ExternalInput").ap()
    ones_d = nc.dram_tensor("onesb", [16, 512], BF16, kind="ExternalInput").ap()
    out_d = nc.dram_tensor("out", [S, H], F32, kind="ExternalOutput").ap()
    sc_oh = nc.dram_tensor("sc_oh", [16 * 128, 2048], BF16).ap()
    sc_ol = nc.dram_tensor("sc_ol", [16 * 128, 2048], BF16).ap()
    sc_r2 = nc.dram_tensor("sc_r2", [S, H], F32).ap()
    sc_b = nc.dram_tensor("sc_b", [16, 256], BF16).ap()
    sc_ws = nc.dram_tensor("sc_ws", [16, 256], BF16).ap()
    sc_wq = nc.dram_tensor("sc_wq", [S, H], F32).ap()

    def sb(name, shape, dt=F32):
        return nc.alloc_sbuf_tensor(name, shape, dt)

    lnxh = sb("lnxh", [128, NC_ * S], BF16)
    lnxl = sb("lnxl", [128, NC_ * S], BF16)
    wrow = sb("wrow", [128, H], F32)
    wqt = [sb(f"wqt{i}", [128, H], BF16) for i in range(4)]
    qh = sb("qh", [128, S], BF16)
    ql = sb("ql", [128, S], BF16)
    kh = sb("kh", [128, S], BF16)
    kl = sb("kl", [128, S], BF16)
    vtok = sb("vtok", [128, S], F32)
    pbuf = sb("pbuf", [128, S], F32)
    ptb = sb("ptb", [128, S], F32)
    oth = sb("oth", [128, 512], BF16)
    otl = sb("otl", [128, 512], BF16)
    fp0 = sb("fp0", [128, 512], F32)
    fp1 = sb("fp1", [128, 512], F32)
    fpu = sb("fpu", [128, 512], BF16)
    bpc = sb("bpc", [128, 512], F32)
    st = sb("st", [128, 64], F32)
    st2 = sb("st2", [128, 64], F32)
    st3 = sb("st3", [128, 64], F32)
    idf = sb("idf_s", [128, 128], F32)
    onesb = sb("onesb_s", [16, 512], BF16)
    bc4 = sb("bc4", [128, 64], F32)
    brow = sb("brow", [1, 1024], BF16)
    brow2 = sb("brow2", [1, 1024], BF16)

    psA = nc.alloc_psum_tensor("psA", [128, 2048], F32)
    psB = nc.alloc_psum_tensor("psB", [128, 2048], F32)
    ps3A = psA.ap().rearrange("p (c n) -> p c n", c=16)
    ps3B = psB.ap().rearrange("p (c n) -> p c n", c=16)
    lnh3 = lnxh.ap().rearrange("p (c n) -> p c n", c=NC_)
    lnl3 = lnxl.ap().rearrange("p (c n) -> p c n", c=NC_)

    p = Prog(nc)

    def K(*a):
        return tuple(a)

    PSa = [K("psA", j) for j in range(4)]
    PSb = [K("psB", j) for j in range(4)]

    def dma_in(dst_ap, src_ap, writes, reads=()):
        p.op("sync", lambda e, d=dst_ap, s=src_ap: e.dma_start(d, s),
             reads=reads, writes=writes, dma=True)

    def dma_out(dst_ap, src_ap, reads, writes=()):
        p.op("gpsimd", lambda e, d=dst_ap, s=src_ap: e.dma_start(d, s),
             reads=reads, writes=writes, dma=True)

    def mm(out_ap, lhsT_ap, rhs_ap, start, stop, reads, writes):
        p.op("tensor", lambda e, o=out_ap, l=lhsT_ap, r=rhs_ap, a=start, b=stop:
             e.matmul(o, l, r, start=a, stop=b), reads=reads, writes=writes)

    def tr(out_ap, in_ap, reads, writes):
        p.op("tensor", lambda e, o=out_ap, i=in_ap: e.transpose(o, i, idf[:, :]),
             reads=list(reads) + [K("idf")], writes=writes)

    def vcopy(out_ap, in_ap, reads, writes):
        p.op("vector", lambda e, o=out_ap, i=in_ap: e.tensor_copy(o, i),
             reads=reads, writes=writes)

    def scopy(out_ap, in_ap, reads, writes):
        p.op("scalar", lambda e, o=out_ap, i=in_ap: e.copy(o, i),
             reads=reads, writes=writes)

    def vsub(out_ap, a_ap, b_ap, reads, writes):
        p.op("vector", lambda e, o=out_ap, a=a_ap, b=b_ap: e.tensor_sub(o, a, b),
             reads=reads, writes=writes)

    def vmul(out_ap, a_ap, b_ap, reads, writes):
        p.op("vector", lambda e, o=out_ap, a=a_ap, b=b_ap: e.tensor_mul(o, a, b),
             reads=reads, writes=writes)

    def vadd(out_ap, a_ap, b_ap, reads, writes):
        p.op("vector", lambda e, o=out_ap, a=a_ap, b=b_ap: e.tensor_add(o, a, b),
             reads=reads, writes=writes)

    def stt(out_ap, in0_ap, scalar, in1_ap, op0, op1, reads, writes):
        p.op("vector", lambda e, o=out_ap, i0=in0_ap, sc=scalar, i1=in1_ap, a=op0, b=op1:
             e.scalar_tensor_tensor(o, i0, sc, i1, a, b), reads=reads, writes=writes)

    def ts(out_ap, in_ap, s1, s2, op0, op1, reads, writes, accum=None):
        if accum is None:
            p.op("vector", lambda e, o=out_ap, i=in_ap, a=s1, b=s2, x=op0, y=op1:
                 e.tensor_scalar(o, i, a, b, x, y), reads=reads, writes=writes)
        else:
            p.op("vector", lambda e, o=out_ap, i=in_ap, a=s1, b=s2, x=op0, y=op1, ac=accum:
                 e.tensor_scalar(o, i, a, b, x, y, accum_out=ac), reads=reads, writes=writes)

    def act(out_ap, in_ap, func, bias, scale, reads, writes, accum=None):
        if accum is None:
            p.op("scalar", lambda e, o=out_ap, i=in_ap, f=func, b=bias, s=scale:
                 e.activation(o, i, f, bias=b, scale=s), reads=reads, writes=writes)
        else:
            p.op("scalar", lambda e, o=out_ap, i=in_ap, f=func, b=bias, s=scale, ac=accum:
                 e.activation(o, i, f, bias=b, scale=s, accum_out=ac),
                 reads=reads, writes=writes)

    def reduce_absmax(out_ap, in_ap, reads, writes):
        p.op("vector", lambda e, o=out_ap, i=in_ap: e.reduce_max(
             o, i, axis=AX.X, apply_absolute_value=True), reads=reads, writes=writes)

    def recip(out_ap, in_ap, reads, writes):
        p.op("vector", lambda e, o=out_ap, i=in_ap: e.reciprocal(o, i),
             reads=reads, writes=writes)

    VELT = float(2 ** 29 + 1)

    def r32_guard(A, ka, B, kb, C, kc):
        p.op("scalar", lambda e, o=B, i=A: e.activation(
             o, i, AF.Identity, bias=0.0, scale=VELT), reads=[ka], writes=[kb])
        p.op("vector", lambda e, o=C, x=B, y=A: e.tensor_sub(o, x, y),
             reads=[kb, ka], writes=[kc])
        p.op("vector", lambda e, o=B, x=B, y=C: e.tensor_sub(o, x, y),
             reads=[kb, kc], writes=[kb])
        p.op("vector", lambda e, o=B, x=B, y=A: e.tensor_sub(o, x, y),
             reads=[kb, ka], writes=[kb])
        p.op("scalar", lambda e, o=C, i=A: e.activation(
             o, i, AF.Abs, bias=0.0, scale=float(2.0 ** -23)),
             reads=[ka], writes=[kc])
        p.op("vector", lambda e, o=B, x=B, y=C: e.tensor_tensor(o, x, y, ALU.min),
             reads=[kb, kc], writes=[kb])
        ts(C, C, -1.0, None, ALU.mult, ALU.bypass, reads=[kc], writes=[kc])
        p.op("vector", lambda e, o=B, x=B, y=C: e.tensor_tensor(o, x, y, ALU.max),
             reads=[kb, kc], writes=[kb])
        p.op("vector", lambda e, o=A, x=A, y=B: e.tensor_add(o, x, y),
             reads=[ka, kb], writes=[ka])

    dma_in(idf[:, :], idf_d[:, :], writes=[K("idf")])
    dma_in(onesb[:, :], ones_d[:, :], writes=[K("onesb")])
    for bi, nm in enumerate(("bq", "bk", "bv", "bo")):
        bl = bpc[0:16, 0:128]
        dma_in(bl, b_d[nm].rearrange("(a b) -> a b", b=128), writes=[K("bpc")])
        p.op("tensor", lambda e, o=psA[:, 0:16], i=bl: e.transpose(
             o, i, idf[0:16, 0:16]), reads=[K("bpc"), K("idf")], writes=[PSa[0]])
        vcopy(bc4[:, 16 * bi:16 * bi + 16], psA[:, 0:16],
              reads=[PSa[0]], writes=[K("bc4", bi)])
    blf = bpc[0:16, 0:128]
    dma_in(blf, b_d["bf"].rearrange("(a b) -> a b", b=128), writes=[K("bpc")])
    vcopy(fpu[0:16, 0:128], blf, reads=[K("bpc")], writes=[K("fpu")])
    p.op("vector", lambda e, o=fpu[0:16, 128:256], a=blf, b=fpu[0:16, 0:128]:
         e.tensor_sub(o, a, b), reads=[K("bpc"), K("fpu")], writes=[K("fpu")])
    dma_out(sc_b[:, :], fpu[0:16, 0:256], reads=[K("fpu")], writes=[K("sc_b")])

    def bias_col(bi, c):
        return bc4[:, 16 * bi + c:16 * bi + c + 1]

    p.op("gpsimd", lambda e: e.memset(bpc[:, 0:128], 0.0), writes=[K("bpc")])

    _slot = [0]

    def ln_stats(src_buf, read_keys):
        sl = _slot[0] % 2
        _slot[0] += 1
        bno = st2[:, 24 * sl:24 * sl + 24]
        for i in range(4):
            p.op("vector", lambda e, o=st2[:, 24 * sl + 6 * i:24 * sl + 6 * i + 6],
                 s=src_buf[:, 512 * i:512 * (i + 1)]: e.bn_stats(o, s),
                 reads=read_keys, writes=[K("bn", sl, i)])
        agg = st[:, 8 * sl:8 * sl + 2]
        p.op("vector", lambda e, o=agg, i=bno: e.bn_aggr(o, i),
             reads=[K("bn", sl, i) for i in range(4)], writes=[K("lnst", sl, "mv")])
        mean = agg[:, 0:1]
        var = agg[:, 1:2]
        t_ = st[:, 8 * sl + 2:8 * sl + 3]
        ts(t_, var, 1.0, EPS, ALU.mult, ALU.add,
           reads=[K("lnst", sl, "mv")], writes=[K("lnst", sl, "t")])
        lnt = st[:, 8 * sl + 3:8 * sl + 4]
        act(lnt, t_, AF.Ln, 0.0, 1.0, reads=[K("lnst", sl, "t")],
            writes=[K("lnst", sl, "lnt")])
        r0 = st[:, 8 * sl + 4:8 * sl + 5]
        act(r0, lnt, AF.Exp, 0.0, -0.5, reads=[K("lnst", sl, "lnt")],
            writes=[K("lnst", sl, "r0")])
        u_ = st[:, 8 * sl + 5:8 * sl + 6]
        vmul(u_, r0, r0, reads=[K("lnst", sl, "r0")], writes=[K("lnst", sl, "u")])
        vmul(u_, u_, t_, reads=[K("lnst", sl, "u"), K("lnst", sl, "t")],
             writes=[K("lnst", sl, "u")])
        ts(u_, u_, -0.5, 1.5, ALU.mult, ALU.add,
           reads=[K("lnst", sl, "u")], writes=[K("lnst", sl, "u")])
        r1 = st[:, 8 * sl + 6:8 * sl + 7]
        vmul(r1, r0, u_, reads=[K("lnst", sl, "r0"), K("lnst", sl, "u")],
             writes=[K("lnst", sl, "r1")])
        nb = st2[:, 48 + sl:49 + sl]
        vmul(nb, mean, r1, reads=[K("lnst", sl, "mv"), K("lnst", sl, "r1")],
             writes=[K("lnst", sl, "nb")])
        ts(nb, nb, -1.0, None, ALU.mult, ALU.bypass,
           reads=[K("lnst", sl, "nb")], writes=[K("lnst", sl, "nb")])
        return r1, nb, [K("lnst", sl, "r1"), K("lnst", sl, "nb")]

    def transpose_split(src_buf, src_keys, col, pp, dsth3=lnh3, dstl3=lnl3,
                        hname="lnxh", lname="lnxl"):
        ps = psA if pp == 0 else psB
        ps3 = ps3A if pp == 0 else ps3B
        pk = PSa if pp == 0 else PSb
        for c in range(16):
            tr(ps[:, 128 * c:128 * (c + 1)], src_buf[:, 128 * c:128 * (c + 1)],
               reads=src_keys, writes=[pk[c // 4]])
        for b in range(4):
            dh = dsth3[:, 4 * b:4 * b + 4, 128 * col:128 * (col + 1)]
            dl = dstl3[:, 4 * b:4 * b + 4, 128 * col:128 * (col + 1)]
            sp = ps3[:, 4 * b:4 * b + 4, :]
            hk = [K(hname, 4 * b + j, col) for j in range(4)]
            lk = [K(lname, 4 * b + j, col) for j in range(4)]
            p.op("scalar", lambda e, o=dh, i=sp: e.copy(o, i),
                 reads=[pk[b]], writes=hk)
            p.op("vector", lambda e, o=dl, i=sp, hh=dh: e.scalar_tensor_tensor(
                 o, i, 1.0, hh, ALU.mult, ALU.subtract),
                 reads=[pk[b]] + hk, writes=lk)

    PBUF_ALL = [K("pbuf")] + [K("pbuf", j) for j in range(4)]
    PTB_ALL = [K("ptb")] + [K("ptb", j) for j in range(4)]
    for tt in range(NT):
        dma_in(pbuf[:, :], xin[128 * tt:128 * (tt + 1), :], writes=PBUF_ALL)
        s_ap, b_ap, skeys = ln_stats(pbuf, [K("pbuf")])
        act(ptb[:, :], pbuf[:, :], AF.Identity, b_ap, s_ap,
            reads=[K("pbuf")] + skeys, writes=PTB_ALL)
        transpose_split(ptb, [K("ptb")], tt, tt % 2)

    LNX_ALL = [K("lnxh", c, t) for c in range(NC_) for t in range(NT)] + \
              [K("lnxl", c, t) for c in range(NC_) for t in range(NT)]

    def lnx_keys(g):
        ks = []
        for c in range(NC_):
            for t_ in range(4 * g, 4 * g + 4):
                ks.append(K("lnxh", c, t_))
                ks.append(K("lnxl", c, t_))
        return ks

    LNXG = [lnx_keys(g) for g in range(NG)]

    def wprep(dram, c, idx_h, idx_l, pp):
        dma_in(wrow[:, 0:1024], dram[128 * c:128 * (c + 1), 0:1024],
               writes=[K("wrow", 0)])
        dma_in(wrow[:, 1024:2048], dram[128 * c:128 * (c + 1), 1024:2048],
               writes=[K("wrow", 1)])
        ps = psA if pp == 0 else psB
        pk = PSa if pp == 0 else PSb
        for j in range(16):
            tr(ps[:, 128 * j:128 * (j + 1)], wrow[:, 128 * j:128 * (j + 1)],
               reads=[K("wrow", j // 8)], writes=[pk[j // 4]])
        dh, dl = wqt[idx_h], wqt[idx_l]
        for pc in range(4):
            sl = slice(512 * pc, 512 * (pc + 1))
            scopy(dh[:, sl], ps[:, sl], reads=[pk[pc]],
                  writes=[K("wqt", idx_h, pc)])
            p.op("vector", lambda e, o=dl[:, sl], i=ps[:, sl], hh=dh[:, sl]:
                 e.scalar_tensor_tensor(o, i, 1.0, hh, ALU.mult, ALU.subtract),
                 reads=[pk[pc], K("wqt", idx_h, pc)],
                 writes=[K("wqt", idx_l, pc)])

    def proj_tile(dst, dk, g, idx_h, idx_l):
        rk = LNXG[g]
        for c in range(NC_):
            lh = lnh3[:, c, 512 * g:512 * (g + 1)]
            ll = lnl3[:, c, 512 * g:512 * (g + 1)]
            whs = wqt[idx_h][:, 128 * c:128 * (c + 1)]
            wls = wqt[idx_l][:, 128 * c:128 * (c + 1)]
            kh_ = K("wqt", idx_h, c // 4)
            kl_ = K("wqt", idx_l, c // 4)
            mm(dst, whs, lh, c == 0, False, reads=[kh_] + (rk if c == 0 else []),
               writes=[dk])
            mm(dst, whs, ll, False, False, reads=[kh_], writes=[dk])
            mm(dst, wls, lh, False, c == NC_ - 1, reads=[kl_], writes=[dk])

    def make_w_pass1(h):
        ems = []
        wam = st3[:, 48:49]
        for pc in range(8):
            def em(pc=pc, h=h):
                land = bpc[:, 256:512]
                dma_in(land, w_d["wf"][128 * h:128 * (h + 1), 256 * pc:256 * (pc + 1)],
                       writes=[K("bpcw")])
                if pc == 0:
                    reduce_absmax(wam, land, reads=[K("bpcw")], writes=[K("wam")])
                else:
                    amp = st3[:, 53:54]
                    reduce_absmax(amp, land, reads=[K("bpcw")], writes=[K("amp")])
                    p.op("vector", lambda e, o=wam, a=wam, b=amp:
                         e.tensor_tensor(o, a, b, ALU.max),
                         reads=[K("wam"), K("amp")], writes=[K("wam")])
                if pc == 7:
                    wscol = st3[:, h:h + 1]
                    ts(wscol, wam, 1e-8, 1.0 / 127.0, ALU.max, ALU.mult,
                       reads=[K("wam")], writes=[K("ws", h)])
                    wrc = st3[:, 49:50]
                    recip(wrc, wscol, reads=[K("ws", h)], writes=[K("wrc")])
                    r32_guard(wscol, K("ws", h), st3[:, 51:52], K("sc51"),
                              st3[:, 52:53], K("sc52"))
            ems.append(em)
        return ems

    def make_w_pass2(h):
        ems = []
        wrc = st3[:, 49:50]
        for pc in range(8):
            def em(pc=pc, h=h):
                land = bpc[:, 256:512]
                A, B = fp1[:, 0:256], fp1[:, 256:512]
                U = fpu[:, 0:256]
                dma_in(land, w_d["wf"][128 * h:128 * (h + 1), 256 * pc:256 * (pc + 1)],
                       writes=[K("bpcw")])
                ts(A, land, wrc, 128.5, ALU.mult, ALU.add,
                   reads=[K("bpcw"), K("wrc")], writes=[K("fp1")])
                scopy(B, A, reads=[K("fp1")], writes=[K("fp1")])
                for hv in PEEL:
                    ts(U, B, hv, hv, ALU.is_ge, ALU.mult,
                       reads=[K("fp1")], writes=[K("fpu")])
                    vsub(B, B, U, reads=[K("fp1"), K("fpu")], writes=[K("fp1")])
                p.op("vector", lambda e, o=B, i=A, i1=B:
                     e.scalar_tensor_tensor(o, i, -128.0, i1, ALU.add, ALU.subtract),
                     reads=[K("fp1")], writes=[K("fp1")])
                dma_out(sc_wq[128 * h:128 * (h + 1), 256 * pc:256 * (pc + 1)], B,
                        reads=[K("fp1")], writes=[K("sc_wq", h, pc // 2)])
            ems.append(em)
        return ems

    qcol = [0]
    for h in range(NH):
        pair_a = (0, 1) if h % 2 == 0 else (2, 3)
        pair_b = (2, 3) if h % 2 == 0 else (0, 1)
        _wp1 = make_w_pass1(h)
        _wp2 = make_w_pass2(h)
        wprep(w_d["wv"], h, pair_a[0], pair_a[1], 0)
        for g in range(NG):
            ps = psA if g % 2 == 0 else psB
            pk = PSa if g % 2 == 0 else PSb
            dst = ps[:, 0:512]
            proj_tile(dst, pk[0], g, pair_a[0], pair_a[1])
            stg, stgk = (fp0, K("fp0")) if g % 2 == 0 else (fp1, K("fp1"))
            act(stg[:, :], dst, AF.Identity, bias_col(2, h), 1.0,
                reads=[pk[0], K("bc4", 2)], writes=[stgk])
            ps2 = psB if g % 2 == 0 else psA
            pk2 = PSb if g % 2 == 0 else PSa
            for j in range(4):
                tr(ps2[:, 512 + 128 * j:512 + 128 * (j + 1)], stg[:, 128 * j:128 * (j + 1)],
                   reads=[stgk], writes=[pk2[1]])
            vcopy(vtok[:, 512 * g:512 * (g + 1)], ps2[:, 512:1024],
                  reads=[pk2[1]], writes=[K("vtok", g)])
            _wp1[2 * g]()
            _wp1[2 * g + 1]()
        wprep(w_d["wq"], h, pair_b[0], pair_b[1], 0)
        for g in range(NG):
            ps = psA if g % 2 == 0 else psB
            pk = PSa if g % 2 == 0 else PSb
            dst = ps[:, 0:512]
            proj_tile(dst, pk[0], g, pair_b[0], pair_b[1])
            act(qh[:, 512 * g:512 * (g + 1)], dst, AF.Identity, bias_col(0, h), 1.0,
                reads=[pk[0], K("bc4", 0)], writes=[K("qh", g)])
            p.op("vector", lambda e, o=ql[:, 512 * g:512 * (g + 1)], i=dst,
                 hh=qh[:, 512 * g:512 * (g + 1)], bb=bias_col(0, h):
                 e.scalar_tensor_tensor(o, i, bb, hh, ALU.add, ALU.subtract),
                 reads=[pk[0], K("qh", g), K("bc4", 0)], writes=[K("ql", g)])
            _wp2[g]()
        wprep(w_d["wk"], h, pair_a[0], pair_a[1], 1)
        for g in range(NG):
            ps = psA if g % 2 == 0 else psB
            pk = PSa if g % 2 == 0 else PSb
            dst = ps[:, 0:512]
            proj_tile(dst, pk[0], g, pair_a[0], pair_a[1])
            act(kh[:, 512 * g:512 * (g + 1)], dst, AF.Identity, bias_col(1, h), 1.0,
                reads=[pk[0], K("bc4", 1)], writes=[K("kh", g)])
            p.op("vector", lambda e, o=kl[:, 512 * g:512 * (g + 1)], i=dst,
                 hh=kh[:, 512 * g:512 * (g + 1)], bb=bias_col(1, h):
                 e.scalar_tensor_tensor(o, i, bb, hh, ALU.add, ALU.subtract),
                 reads=[pk[0], K("kh", g), K("bc4", 1)], writes=[K("kl", g)])
            _wp2[4 + g]()

        for qt in range(NT):
            ps = psA if qt % 2 == 0 else psB
            pk = PSa if qt % 2 == 0 else PSb
            lq_h = qh[:, 128 * qt:128 * (qt + 1)]
            lq_l = ql[:, 128 * qt:128 * (qt + 1)]
            qkeys = [K("qh", qt // 4), K("ql", qt // 4)]
            c0 = qcol[0] % 8
            qcol[0] += 1
            nm = st[:, 24 + c0:25 + c0]
            rc = st[:, 40 + c0:41 + c0]
            nmj = [st2[:, 50 + jj:51 + jj] for jj in range(4)]
            smj = [st2[:, 54 + jj:55 + jj] for jj in range(4)]
            for j in range(NG):
                dst = ps[:, 512 * j:512 * (j + 1)]
                rh = kh[:, 512 * j:512 * (j + 1)]
                rl = kl[:, 512 * j:512 * (j + 1)]
                kkeys = [K("kh", j), K("kl", j)]
                mm(dst, lq_h, rh, True, False, reads=qkeys + kkeys, writes=[pk[j]])
                mm(dst, lq_h, rl, False, False, reads=[], writes=[pk[j]])
                mm(dst, lq_l, rh, False, True, reads=[], writes=[pk[j]])
                ts(pbuf[:, 512 * j:512 * (j + 1)], dst, -SCALE, 3.0e38,
                   ALU.mult, ALU.min, reads=[pk[j]],
                   writes=[K("pbuf", j), K("nmj", j)], accum=nmj[j])
            p.op("vector", lambda e, o=nm, a=nmj[0], b=nmj[1]: e.tensor_tensor(
                 o, a, b, ALU.min), reads=[K("nmj", 0), K("nmj", 1)],
                 writes=[K("nm", c0)])
            p.op("vector", lambda e, o=nm, a=nm, b=nmj[2]: e.tensor_tensor(
                 o, a, b, ALU.min), reads=[K("nm", c0), K("nmj", 2)],
                 writes=[K("nm", c0)])
            p.op("vector", lambda e, o=nm, a=nm, b=nmj[3]: e.tensor_tensor(
                 o, a, b, ALU.min), reads=[K("nm", c0), K("nmj", 3)],
                 writes=[K("nm", c0)])
            for j in range(NG):
                act(pbuf[:, 512 * j:512 * (j + 1)], ps[:, 512 * j:512 * (j + 1)],
                    AF.Exp, nm, SCALE,
                    reads=[pk[j], K("nm", c0)],
                    writes=[K("pbuf", j), K("smj", j)], accum=smj[j])
            p.op("vector", lambda e, o=rc, a=smj[0], b=smj[1]: e.tensor_add(o, a, b),
                 reads=[K("smj", 0), K("smj", 1)], writes=[K("rc", c0)])
            p.op("vector", lambda e, o=rc, a=rc, b=smj[2]: e.tensor_add(o, a, b),
                 reads=[K("rc", c0), K("smj", 2)], writes=[K("rc", c0)])
            p.op("vector", lambda e, o=rc, a=rc, b=smj[3]: e.tensor_add(o, a, b),
                 reads=[K("rc", c0), K("smj", 3)], writes=[K("rc", c0)])
            recip(rc, rc, reads=[K("rc", c0)], writes=[K("rc", c0)])
            for j in range(16):
                tr(ps[:, 128 * j:128 * (j + 1)], pbuf[:, 128 * j:128 * (j + 1)],
                   reads=[K("pbuf", j // 4)], writes=[pk[j // 4]])
            for j in range(NG):
                scopy(ptb[:, 512 * j:512 * (j + 1)], ps[:, 512 * j:512 * (j + 1)],
                      reads=[pk[j]], writes=[K("ptb"), K("ptb", j)] if j == 0
                      else [K("ptb", j)])
            vcopy(bpc[:, 0:1], rc, reads=[K("rc", c0)], writes=[K("bpc")])
            p.op("tensor", lambda e, o=ps[:, 256:384], i=bpc[:, 0:128]:
                 e.transpose(o, i, idf[:, :]),
                 reads=[K("bpc"), K("idf")], writes=[pk[0]])
            vcopy(brow2[0:1, 0:128], ps[0:1, 256:384], reads=[pk[0]],
                  writes=[K("brow2")])
            p.op("vector", lambda e, o=brow2[0:1, 128:256], i=ps[0:1, 256:384],
                 hh=brow2[0:1, 0:128]: e.scalar_tensor_tensor(
                 o, i, 1.0, hh, ALU.mult, ALU.subtract),
                 reads=[pk[0], K("brow2")], writes=[K("brow2")])
            mm(ps[:, 128:256], onesb[0:1, 0:128], brow2[0:1, 0:128], True, False,
               reads=[K("onesb"), K("brow2")], writes=[pk[0]])
            mm(ps[:, 128:256], onesb[0:1, 0:128], brow2[0:1, 128:256], False, True,
               reads=[K("brow2")], writes=[pk[0]])
            vcopy(bpc[:, 128:256], ps[:, 128:256], reads=[pk[0]], writes=[K("bpcrc")])
            first = True
            for j in range(NC_):
                mm(ps[:, 0:128], vtok[:, 128 * j:128 * (j + 1)],
                   ptb[:, 128 * j:128 * (j + 1)], first, j == NC_ - 1,
                   reads=([K("vtok", jj) for jj in range(4)] if first else []) +
                         ([K("ptb", j // 4)] if j % 4 == 0 else []),
                   writes=[pk[0]])
                first = False
            q4 = qt % 4
            p.op("vector", lambda e, o=fp0[:, 0:128], a=ps[:, 0:128],
                 b=bpc[:, 128:256]: e.tensor_mul(o, a, b),
                 reads=[pk[0], K("bpcrc")], writes=[K("fp0")])
            vcopy(oth[:, 128 * q4:128 * (q4 + 1)], fp0[:, 0:128],
                  reads=[K("fp0")], writes=[K("oth", q4)])
            p.op("vector", lambda e, o=otl[:, 128 * q4:128 * (q4 + 1)],
                 i=fp0[:, 0:128], hh=oth[:, 128 * q4:128 * (q4 + 1)]:
                 e.scalar_tensor_tensor(o, i, 1.0, hh, ALU.mult, ALU.subtract),
                 reads=[K("fp0"), K("oth", q4)], writes=[K("otl", q4)])
            if q4 == 3:
                gq = qt // 4
                blk = (4 * gq + h // 4) * 128
                cs = 512 * (h % 4)
                dma_out(sc_oh[blk:blk + 128, cs:cs + 512],
                        oth[:, :], reads=[K("oth", i) for i in range(4)],
                        writes=[K("sc_oh", gq, h)])
                dma_out(sc_ol[blk:blk + 128, cs:cs + 512],
                        otl[:, :], reads=[K("otl", i) for i in range(4)],
                        writes=[K("sc_ol", gq, h)])

    for oc in range(NC_):
        dma_in(wrow[:, :], w_d["wo"][128 * oc:128 * (oc + 1), :], writes=[K("wrow"), K("wrow", 0), K("wrow", 1)])
        transpose_split(wrow, [K("wrow")], oc, oc % 2)

    ya = [pbuf, ptb, vtok, wrow]
    yak = ["pbuf", "ptb", "vtok", "wrow"]
    ogb = [qh, ql, kh, kl]
    ogbk = ["qh", "ql", "kh", "kl"]

    def ya_keys(t_):
        if yak[t_] == "pbuf":
            return [K("pbuf")] + [K("pbuf", j) for j in range(4)]
        if yak[t_] == "wrow":
            return [K("wrow"), K("wrow", 0), K("wrow", 1)]
        if yak[t_] == "ptb":
            return [K("ptb")]
        return [K("vtok", i) for i in range(4)]

    for g in range(NG):
        for b in range(4):
            blk = (4 * g + b) * 128
            dma_in(ogb[b][:, :], sc_oh[blk:blk + 128, :],
                   writes=[K(ogbk[b], q) for q in range(4)],
                   reads=[K("sc_oh", g, 4 * b + q) for q in range(4)])
            dma_in(wqt[b][:, :], sc_ol[blk:blk + 128, :],
                   writes=[K("wqt", b, q) for q in range(4)],
                   reads=[K("sc_ol", g, 4 * b + q) for q in range(4)])

        def oglo_key(c):
            return K("wqt", c // 4, c % 4)

        for ocg in range(4):
            for oc2 in range(4):
                oc = 4 * ocg + oc2
                dst = psA[:, 512 * (oc2 % 2):512 * (oc2 % 2) + 512]
                dk = PSa[oc2 % 2]
                wo_keys = [K("lnxh", c_, oc) for c_ in range(NC_)] + \
                          [K("lnxl", c_, oc) for c_ in range(NC_)]
                for c in range(NC_):
                    ah = ogb[c // 4][:, 512 * (c % 4):512 * (c % 4 + 1)]
                    al_ = wqt[c // 4][:, 512 * (c % 4):512 * (c % 4 + 1)]
                    akeys = [K(ogbk[c // 4], c % 4), oglo_key(c)]
                    wh_ = lnh3[:, c, 128 * oc:128 * (oc + 1)]
                    wl_ = lnl3[:, c, 128 * oc:128 * (oc + 1)]
                    mm(dst, wh_, ah, c == 0, False,
                       reads=(wo_keys if c == 0 else []) + akeys, writes=[dk])
                    mm(dst, wh_, al_, False, False, reads=[], writes=[dk])
                    mm(dst, wl_, ah, False, c == NC_ - 1, reads=[], writes=[dk])
                stg, stgk = (fp0, K("fp0")) if oc2 % 2 == 0 else (fp1, K("fp1"))
                act(stg[:, :], dst, AF.Identity, bias_col(3, oc), 1.0,
                    reads=[dk, K("bc4", 3)], writes=[stgk])
                for t_ in range(4):
                    tr(psB[:, 512 * t_ + 128 * oc2:512 * t_ + 128 * (oc2 + 1)],
                       stg[:, 128 * t_:128 * (t_ + 1)], reads=[stgk], writes=[PSb[t_]])
            for t_ in range(4):
                vcopy(ya[t_][:, 512 * ocg:512 * (ocg + 1)],
                      psB[:, 512 * t_:512 * (t_ + 1)],
                      reads=[PSb[t_]], writes=ya_keys(t_))
        for t_ in range(4):
            tt = 4 * g + t_
            yk = ya_keys(t_)
            s_ap, b_ap, skeys = ln_stats(ya[t_], yk)
            for h2 in range(2):
                sl = slice(1024 * h2, 1024 * (h2 + 1))
                bks = [PSb[2 * h2], PSb[2 * h2 + 1]]
                act(psB[:, sl], ya[t_][:, sl], AF.Identity, b_ap, s_ap,
                    reads=yk + skeys, writes=bks)
                dma_in(ya[t_][:, sl], xin[128 * tt:128 * (tt + 1), sl],
                       writes=yk, reads=bks)
                p.op("vector", lambda e, o=ya[t_][:, sl], a=ya[t_][:, sl],
                     b=psB[:, sl]: e.tensor_add(o, a, b),
                     reads=yk + bks, writes=yk)
                dma_out(sc_r2[128 * tt:128 * (tt + 1), sl], ya[t_][:, sl],
                        reads=yk, writes=[K("sc_r2", tt, 2 * h2), K("sc_r2", tt, 2 * h2 + 1)])

    def quant_peel(src_buf, src_keys, rcol, rkey, col, hname_lname, pp,
                   eng="vector", bufs=None):
        d3 = lnh3 if hname_lname == "lnxh" else lnl3
        ps = psA if pp == 0 else psB
        ps_ = ps3A if pp == 0 else ps3B
        pk = PSa if pp == 0 else PSb
        if bufs is None:
            A, kA, B, kB, U, kU, W_ = fp0, K("fp0"), fp1, K("fp1"), fpu, K("fpu"), 512
        else:
            A, kA, B, kB, U, kU, W_ = bufs
        npc = 2048 // W_

        def tseng(out_ap, in_ap, s1, s2, op0, op1, reads, writes):
            p.op(eng, lambda e, o=out_ap, i=in_ap, a=s1, b=s2, x=op0, y=op1:
                 e.tensor_scalar(o, i, a, b, x, y), reads=reads, writes=writes)

        for pc in range(npc):
            sl = src_buf[:, W_ * pc:W_ * (pc + 1)]
            Asl, Bsl, Usl = A[:, 0:W_], B[:, 0:W_], U[:, 0:W_]
            tseng(Asl, sl, rcol, 128.5, ALU.mult, ALU.add,
                  reads=list(src_keys) + [rkey], writes=[kA])
            p.op("scalar" if eng == "vector" else eng,
                 lambda e, o=Bsl, i=Asl: e.copy(o, i) if hasattr(e, "copy")
                 else e.tensor_copy(o, i), reads=[kA], writes=[kB])
            for hv in PEEL:
                tseng(Usl, Bsl, hv, hv, ALU.is_ge, ALU.mult,
                      reads=[kB], writes=[kU])
                p.op(eng, lambda e, o=Bsl, a=Bsl, b=Usl: e.tensor_sub(o, a, b),
                     reads=[kB, kU], writes=[kB])
            if eng == "vector":
                p.op(eng, lambda e, o=Bsl, i=Asl, i1=Bsl:
                     e.scalar_tensor_tensor(o, i, -128.0, i1, ALU.add, ALU.subtract),
                     reads=[kA, kB], writes=[kB])
            else:
                tseng(Asl, Asl, -128.0, None, ALU.add, ALU.bypass,
                      reads=[kA], writes=[kA])
                p.op(eng, lambda e, o=Bsl, a=Asl, b=Bsl: e.tensor_sub(o, a, b),
                     reads=[kA, kB], writes=[kB])
            for j in range(W_ // 128):
                tr(ps[:, W_ * pc + 128 * j:W_ * pc + 128 * (j + 1)],
                   Bsl[:, 128 * j:128 * (j + 1)], reads=[kB],
                   writes=[pk[(W_ * pc + 128 * j) // 512]])
        for bq in range(4):
            dh = d3[:, 4 * bq:4 * bq + 4, 128 * col:128 * (col + 1)]
            hk = [K(hname_lname, 4 * bq + j, col) for j in range(4)]
            p.op("scalar", lambda e, o=dh, i=ps_[:, 4 * bq:4 * bq + 4, :]:
                 e.copy(o, i), reads=[pk[bq]], writes=hk)

    for oc in range(NC_):
        dma_in(wrow[:, :], sc_wq[128 * oc:128 * (oc + 1), :],
               writes=[K("wrow"), K("wrow", 0), K("wrow", 1)],
               reads=[K("sc_wq", oc, q) for q in range(4)])
        for j in range(16):
            tr(psA[:, 128 * j:128 * (j + 1)], wrow[:, 128 * j:128 * (j + 1)],
               reads=[K("wrow"), K("wrow", 0), K("wrow", 1)],
               writes=[PSa[j // 4]])
        for bq in range(4):
            dh = lnl3[:, 4 * bq:4 * bq + 4, 128 * oc:128 * (oc + 1)]
            hk = [K("lnxl", 4 * bq + j, oc) for j in range(4)]
            p.op("scalar", lambda e, o=dh, i=ps3A[:, 4 * bq:4 * bq + 4, :]:
                 e.copy(o, i), reads=[PSa[bq]], writes=hk)

    p.op("gpsimd", lambda e: e.memset(fp0[:, 0:128], 0.0), writes=[K("fp0")])
    vcopy(fp0[:, 0:16], st3[:, 0:16], reads=[K("ws", oc_) for oc_ in range(NC_)],
          writes=[K("fp0")])
    p.op("tensor", lambda e: e.transpose(psA[:, 0:128], fp0[:, 0:128], idf[:, :]),
         reads=[K("fp0"), K("idf")], writes=[PSa[0]])
    vcopy(fpu[0:16, 0:128], psA[0:16, 0:128], reads=[PSa[0]], writes=[K("fpu")])
    p.op("vector", lambda e: e.scalar_tensor_tensor(
         fpu[0:16, 128:256], psA[0:16, 0:128], 1.0, fpu[0:16, 0:128],
         ALU.mult, ALU.subtract), reads=[PSa[0], K("fpu")], writes=[K("fpu")])
    dma_out(sc_ws[:, :], fpu[0:16, 0:256], reads=[K("fpu")], writes=[K("sc_ws")])

    for tt in range(NT):
        dma_in(pbuf[:, :], sc_r2[128 * tt:128 * (tt + 1), :], writes=PBUF_ALL,
               reads=[K("sc_r2", tt, pc_) for pc_ in range(4)])
        s_ap, b_ap, skeys = ln_stats(pbuf, [K("pbuf")])
        act(ptb[:, :], pbuf[:, :], AF.Identity, b_ap, s_ap,
            reads=[K("pbuf")] + skeys, writes=PTB_ALL)
        xam = st3[:, 50:51]
        reduce_absmax(xam, ptb[:, :], reads=[K("ptb")], writes=[K("xam")])
        xscol = st3[:, 16 + tt:17 + tt]
        ts(xscol, xam, 1.0 / 127.0, 1e-12, ALU.mult, ALU.max,
           reads=[K("xam")], writes=[K("xs", tt)])
        xrcol = st3[:, 32 + tt:33 + tt]
        recip(xrcol, xscol, reads=[K("xs", tt)], writes=[K("xr", tt)])
        quant_peel(ptb, [K("ptb")], xrcol, K("xr", tt), tt, "lnxh", 1)
        xs32col = st[:, 48 + tt:49 + tt]
        vcopy(xs32col, xscol, reads=[K("xs", tt)], writes=[K("xs32", tt)])
        r32_guard(xs32col, K("xs32", tt), st3[:, 53:54], K("sc53"),
                  st3[:, 54:55], K("sc54"))

    for g in range(NG):
        for ocg in range(4):
            dma_in(brow[0:1, :], sc_b[4 * ocg:4 * ocg + 4, :].rearrange("a b -> (a b)").rearrange("(a b) -> a b", a=1),
                   writes=[K("brow")], reads=[K("sc_b")])
            for oc2 in range(4):
                dstb = psA[:, 1536 + 128 * oc2:1536 + 128 * (oc2 + 1)]
                mm(dstb, onesb[0:1, 0:128], brow[0:1, 256 * oc2:256 * oc2 + 128],
                   True, False, reads=[K("onesb"), K("brow")], writes=[PSa[3]])
                mm(dstb, onesb[0:1, 0:128], brow[0:1, 256 * oc2 + 128:256 * oc2 + 256],
                   False, True, reads=[K("onesb"), K("brow")], writes=[PSa[3]])
            vcopy(bpc[:, :], psA[:, 1536:2048], reads=[PSa[3]],
                  writes=[K("bpc"), K("bpcrc"), K("bpcw")])
            dma_in(brow2[0:1, :], sc_ws[4 * ocg:4 * ocg + 4, :].rearrange("a b -> (a b)").rearrange("(a b) -> a b", a=1),
                   writes=[K("brow2")], reads=[K("sc_ws")])
            for oc2 in range(4):
                wdst = psA[:, 1024 + 128 * oc2:1024 + 128 * (oc2 + 1)]
                mm(wdst, onesb[0:1, 0:128], brow2[0:1, 256 * oc2:256 * oc2 + 128],
                   True, False, reads=[K("onesb"), K("brow2")], writes=[PSa[2]])
                mm(wdst, onesb[0:1, 0:128], brow2[0:1, 256 * oc2 + 128:256 * oc2 + 256],
                   False, True, reads=[K("onesb"), K("brow2")], writes=[PSa[2]])
            for oc2 in range(4):
                oc = 4 * ocg + oc2
                dst = psA[:, 512 * (oc2 % 2):512 * (oc2 % 2) + 512]
                dk = PSa[oc2 % 2]
                first = True
                for c in range(NC_):
                    wslice = lnl3[:, c, 128 * oc:128 * (oc + 1)]
                    xslice = lnh3[:, c, 512 * g:512 * (g + 1)]
                    rks = [K("lnxl", c, oc)] + \
                          [K("lnxh", c, 4 * g + i) for i in range(4)]
                    mm(dst, wslice, xslice, first, c == NC_ - 1,
                       reads=rks, writes=[dk])
                    first = False
                stg, stgk = (fp0, K("fp0")) if oc2 % 2 == 0 else (fp1, K("fp1"))
                scopy(stg[:, :], dst, reads=[dk], writes=[stgk])
                for t_ in range(4):
                    tr(psB[:, 512 * t_ + 128 * oc2:512 * t_ + 128 * (oc2 + 1)],
                       stg[:, 128 * t_:128 * (t_ + 1)], reads=[stgk], writes=[PSb[t_]])
            for t_ in range(4):
                tt = 4 * g + t_
                xs32c = st[:, 48 + tt:49 + tt]
                yasl = ya[t_][:, 512 * ocg:512 * (ocg + 1)]
                yk = ya_keys(t_)
                p.op("scalar", lambda e, o=fp0[:, :], i=psB[:, 512 * t_:512 * (t_ + 1)],
                     sc=xs32c: e.activation(o, i, AF.Identity, bias=0.0, scale=sc),
                     reads=[PSb[t_], K("xs32", tt)], writes=[K("fp0")])
                r32_guard(fp0[:, :], K("fp0"), fp1[:, :], K("fp1"), yasl, yk[0])
                p.op("vector", lambda e, o=fp1[:, :], a=fp0[:, :],
                     b=psA[:, 1024:1536]: e.tensor_mul(o, a, b),
                     reads=[K("fp0"), PSa[2]], writes=[K("fp1")])
                r32_guard(fp1[:, :], K("fp1"), fp0[:, :], K("fp0"), yasl, yk[0])
                p.op("vector", lambda e, o=yasl, a=fp1[:, :], b=bpc[:, :]:
                     e.tensor_add(o, a, b),
                     reads=[K("fp1"), K("bpc")], writes=yk)
        for t_ in range(4):
            tt = 4 * g + t_
            yk = ya_keys(t_)
            s_ap, b_ap, skeys = ln_stats(ya[t_], yk)
            for h2 in range(2):
                sl = slice(1024 * h2, 1024 * (h2 + 1))
                bks = [PSb[2 * h2], PSb[2 * h2 + 1]]
                act(psB[:, sl], ya[t_][:, sl], AF.Identity, b_ap, s_ap,
                    reads=yk + skeys, writes=bks)
                dma_in(ya[t_][:, sl], sc_r2[128 * tt:128 * (tt + 1), sl],
                       writes=yk, reads=bks + [K("sc_r2", tt, 2 * h2), K("sc_r2", tt, 2 * h2 + 1)])
                p.op("vector", lambda e, o=ya[t_][:, sl], a=ya[t_][:, sl],
                     b=psB[:, sl]: e.tensor_add(o, a, b),
                     reads=yk + bks, writes=yk)
                dma_out(out_d[128 * tt:128 * (tt + 1), sl], ya[t_][:, sl],
                        reads=yk)

    ng_ = p.dma_idx["gpsimd"]
    p.tail_waits = {"gpsimd": [(f"dma_gpsimd_{i}",
                                ((ng_ - 1 - i) // Prog.NDMA + 1) * 16)
                               for i in range(min(Prog.NDMA, ng_))]}

    import contextlib
    stack = contextlib.ExitStack()
    handles = {}
    for s_ in p.sem_names:
        handles[s_] = stack.enter_context(nc.semaphore(s_))
    p.sems = handles
    with nc.Block() as block:
        p.emit(block)
    stack.close()
    return nc


_CACHE = {}


def kernel(L0_attn_hidden_state, L0_attn_Wq, L0_attn_bq, L0_attn_Wk, L0_attn_bk,
           L0_attn_Wv, L0_attn_bv, L0_attn_Wo, L0_attn_bo, L0_ffn_weight, L0_ffn_bias):
    if "nc" not in _CACHE:
        _CACHE["nc"] = build()
    nc = _CACHE["nc"]
    hs = np.ascontiguousarray(np.asarray(L0_attn_hidden_state, dtype=np.float32))
    common = {
        "wq": np.ascontiguousarray(L0_attn_Wq, dtype=np.float32),
        "wk": np.ascontiguousarray(L0_attn_Wk, dtype=np.float32),
        "wv": np.ascontiguousarray(L0_attn_Wv, dtype=np.float32),
        "wo": np.ascontiguousarray(L0_attn_Wo, dtype=np.float32),
        "wf": np.ascontiguousarray(L0_ffn_weight, dtype=np.float32),
        "bq": np.ascontiguousarray(L0_attn_bq, dtype=np.float32),
        "bk": np.ascontiguousarray(L0_attn_bk, dtype=np.float32),
        "bv": np.ascontiguousarray(L0_attn_bv, dtype=np.float32),
        "bo": np.ascontiguousarray(L0_attn_bo, dtype=np.float32),
        "bf": np.ascontiguousarray(L0_ffn_bias, dtype=np.float32),
        "idf": np.eye(128, dtype=np.float32),
        "onesb": np.ones((16, 512), dtype=ml_dtypes.bfloat16),
    }
    in_maps = [dict(common, x=np.ascontiguousarray(hs[b])) for b in range(8)]
    res = run_bass_kernel_spmd(nc, in_maps, core_ids=list(range(8)))
    out = np.stack([np.asarray(r["out"]) for r in res.results], axis=0)
    return out
